# version 09
# baseline (speedup 1.0000x reference)
import numpy as np
import ml_dtypes
import concourse.bass as bass
import concourse.mybir as mybir
from concourse.bass_utils import run_bass_kernel_spmd
from concourse.alu_op_type import AluOpType as OP
from concourse.bass_isa_rust import ReduceOp

F32 = mybir.dt.float32
BF16 = mybir.dt.bfloat16
I8 = mybir.dt.int8
I16 = mybir.dt.int16
I32 = mybir.dt.int32
U16 = mybir.dt.uint16
AF = mybir.ActivationFunctionType
AX = mybir.AxisListType

D, T, DI, NH, NCORES = 1024, 2048, 3072, 16, 8
MG = 12582912.0
NDC = D // 128
NIC = DI // 128
NTT = T // 512
ISZ = {F32: 4, BF16: 2, I8: 1, I16: 2, I32: 4}
SB_BYTES = 208512


class Arena:
    def __init__(self, n):
        self.n, self.blocks, self.hw = n, [], 0

    def alloc(self, name, size):
        size = -(-size // 256) * 256
        off = 0
        for boff, bsize, _ in sorted(self.blocks):
            if off + size <= boff:
                break
            off = max(off, boff + bsize)
        assert off + size <= self.n, f"arena OOM: {name} {size}B at {off}"
        self.blocks.append((off, size, name))
        self.hw = max(self.hw, off + size)
        return off

    def free(self, *names):
        self.blocks = [b for b in self.blocks if b[2] not in names]


def build_nc():
    nc = bass.Bass("TRN2", target_bir_lowering=False, debug=False)
    nc.detect_race_conditions = False

    xT = nc.dram_tensor("xT", [D, T], F32, kind="ExternalInput").ap()
    wTd = {n: nc.dram_tensor("w" + n + "T", [D, D], F32, kind="ExternalInput").ap()
           for n in ("q", "k", "v", "o")}
    wuT = nc.dram_tensor("wuT", [D, DI], F32, kind="ExternalInput").ap()
    wgT = nc.dram_tensor("wgT", [D, DI], F32, kind="ExternalInput").ap()
    wdT = nc.dram_tensor("wdT", [DI, D], F32, kind="ExternalInput").ap()
    wu_n = nc.dram_tensor("wu_n", [DI, D], F32, kind="ExternalInput").ap()
    wg_n = nc.dram_tensor("wg_n", [DI, D], F32, kind="ExternalInput").ap()
    wd_n = nc.dram_tensor("wd_n", [D, DI], F32, kind="ExternalInput").ap()
    bqpp_d = nc.dram_tensor("bqpp", [128, NDC], F32, kind="ExternalInput").ap()
    bkpp_d = nc.dram_tensor("bkpp", [128, NDC], F32, kind="ExternalInput").ap()
    bopp_d = nc.dram_tensor("bopp", [128, NDC], F32, kind="ExternalInput").ap()
    bvrow_d = nc.dram_tensor("bvrow", [1, D], F32, kind="ExternalInput").ap()
    ident_d = nc.dram_tensor("identc", [128, 128], F32, kind="ExternalInput").ap()
    ind16_d = nc.dram_tensor("ind16c", [128, NDC * 16], BF16, kind="ExternalInput").ap()
    indk_d = nc.dram_tensor("indkc", [96, NIC * 128], F32, kind="ExternalInput").ap()
    outT = nc.dram_tensor("outT", [D, T], F32, kind="ExternalOutput").ap()

    cc_i = nc.dram_tensor("cc_i", [1, 128], F32, kind="Internal")
    cc_o = nc.dram_tensor("cc_o", [1, 128], F32, kind="Internal")
    x2sp = nc.dram_tensor("x2sp", [D, T], F32, kind="Internal").ap()
    y_hp = nc.dram_tensor("y_hp", [D, T], F32, kind="Internal").ap()
    up_hp = nc.dram_tensor("up_hp", [DI, T], BF16, kind="Internal").ap()
    gt_hp = nc.dram_tensor("gt_hp", [DI, T], BF16, kind="Internal").ap()
    gs_hp = nc.dram_tensor("gs_hp", [DI, T], BF16, kind="Internal").ap()
    tp_hp = nc.dram_tensor("tp_hp", [DI, T], BF16, kind="Internal").ap()

    from contextlib import ExitStack
    stack = ExitStack()
    i16t = stack.enter_context(nc.sbuf_tensor("i16t", [128, T], I16))[:]
    sbA = stack.enter_context(nc.sbuf_tensor("A", [128, SB_BYTES], I8))[:]
    A = Arena(SB_BYTES)
    CASTV = {dt: sbA.bitcast(dt) for dt in (F32, BF16, I8, I16, I32)}

    def fv(name, rows, cols, dt=F32):
        o = A.alloc(name, cols * ISZ[dt])
        e = o // ISZ[dt]
        return CASTV[dt][0:rows, e:e + cols]

    SEMS, C = {}, {}
    for s in ("dl", "dg", "pe", "vv", "ss", "gg", "cc", "pr"):
        SEMS[s] = stack.enter_context(nc.semaphore(s))
        C[s] = 0

    block = stack.enter_context(nc.Block())
    SECID = [0]

    def _secwrap(reg):
        def run(f):
            SECID[0] += 1
            reg(f)
        return run

    VEC, SCA, GPS, SYN = (_secwrap(block.vector), _secwrap(block.scalar),
                          _secwrap(block.gpsimd), _secwrap(block.sync))

    def TEN(f):
        pr_target = C["pr"]

        def g(t):
            t.wait_ge(SEMS["pr"], pr_target)
            f(t)
        block.tensor(g)

    def I(inst, name, n=1):
        inst.then_inc(SEMS[name], n)
        C[name] += n

    def W(eng, name, val=None):
        v = C[name] if val is None else val
        assert v <= C[name], f"wait on {name} for {v} > emitted {C[name]}"
        eng.wait_ge(SEMS[name], v)

    LASTDMA = {"dl": -1, "dg": -1}

    def dma(eng, out, in_, sem):
        if LASTDMA.get(sem) != SECID[0]:
            eng.wait_ge(SEMS[sem], C[sem])
            LASTDMA[sem] = SECID[0]
        eng.dma_start(out=out, in_=in_).then_inc(SEMS[sem], 16)
        C[sem] += 16

    def PR(eng):
        I(eng.drain(), "pr")

    def red512(v, dst, src, n):
        k = n // 512
        for j in range(k):
            v.tensor_reduce(msl[:, 24 + j:25 + j], src[:, j * 512:(j + 1) * 512],
                            AX.X, OP.max, apply_absolute_value=True)
        v.drain()
        v.tensor_reduce(dst, msl[:, 24:24 + k], AX.X, OP.max)

    def fence():
        return {k: C[k] for k in ("pe", "vv", "ss", "gg")}

    def wfence(eng, f):
        for k, v in f.items():
            eng.wait_ge(SEMS[k], v)

    ones_r = fv("ones_r", 1, 128)
    ones_c = fv("ones_c", 128, 1)
    ident = fv("ident", 128, 128)
    ind16 = fv("ind16", 128, NDC * 16, BF16)
    bqpp = fv("bqpp", 128, NDC)
    bkpp = fv("bkpp", 128, NDC)
    bopp = fv("bopp", 128, NDC)
    bvrow = fv("bvrow", 1, D)
    bvbc = fv("bvbc", 128, D)
    cst = fv("cst", 1, 16)
    sc = fv("sc", 1, 64)
    ccst = fv("ccst", 1, 128)
    ccrd = fv("ccrd", 1, 128)
    bcrow = fv("bcrow", 1, 32)
    bc = fv("bc", 128, 32)
    msl = fv("msl", 128, 32)
    msr = fv("msr", 1, 32)
    nmcol = fv("nmcol", 16, 8)
    nmrow = fv("nmrow", 1, 16)
    negM = fv("negM", 128, 16)
    eps_t = fv("eps_t", 1, 2)
    qtmp = fv("qtmp", 128, T)

    SLOT, BCJ = {}, {}

    def sl(name):
        j = SLOT.setdefault(name, len(SLOT))
        assert j < 64
        return sc[0:1, j:j + 1]

    def bcj(name):
        j = BCJ.setdefault(name, len(BCJ))
        assert j < 32
        return j

    def bca(name):
        j = bcj(name)
        return bc[:, j:j + 1]

    ps_bc = stack.enter_context(nc.psum_tensor("ps_bc", [128, 160], F32))

    def bc_round(names):
        js = sorted(bcj(n) for n in names)
        j0, j1 = js[0], js[-1] + 1

        def _(v):
            W(v, "vv")
            for n in names:
                v.tensor_copy(bcrow[0:1, bcj(n):bcj(n) + 1], sl(n))
            I(v.drain(), "vv")
        VEC(_)
        s1 = C["vv"]

        def _(t):
            W(t, "vv", s1)
            I(t.matmul(ps_bc[:, j0:j1], ones_r, bcrow[0:1, j0:j1],
                       start=True, stop=True), "pe")
        TEN(_)
        s2 = C["pe"]

        def _(v):
            W(v, "pe", s2)
            v.tensor_copy(bc[:, j0:j1], ps_bc[:, j0:j1])
            PR(v)
            I(v.drain(), "vv")
        VEC(_)
        return C["vv"]

    def collective():
        def _(g):
            W(g, "vv")
            dma(g, cc_i.ap(), ccst, "dg")
            g.wait_ge(SEMS["dg"], C["dg"])
            I(g.collective_compute("AllReduce", OP.max,
                                    replica_groups=[list(range(NCORES))],
                                    ins=[cc_i.ap()], outs=[cc_o.ap()]), "cc")
            g.wait_ge(SEMS["cc"], C["cc"])
            dma(g, ccrd, cc_o.ap(), "dg")
            g.wait_ge(SEMS["dg"], C["dg"])
            I(g.drain(), "cc")
        GPS(_)
        return C["cc"]

    def crossmax(src_j0, n, dst_slot):
        def _(v):
            W(v, "vv")
            v.tensor_reduce(msl[:, 30:31], msl[:, src_j0:src_j0 + n], AX.X, OP.max)
            I(v.drain(), "vv")
        VEC(_)
        s1 = C["vv"]

        def _(t):
            W(t, "vv", s1)
            I(t.transpose(ps_bc[0:1, 32:160], msl[:, 30:31], ident), "pe")
        TEN(_)
        s2 = C["pe"]

        def _(v):
            W(v, "pe", s2)
            v.tensor_reduce(sl(dst_slot), ps_bc[0:1, 32:160], AX.X, OP.max)
            PR(v)
            I(v.drain(), "vv")
        VEC(_)

    A.alloc("xs_pad", 2048)
    xs = fv("xs", 128, NDC * T)
    wstA = fv("wstA", 128, NDC * D)
    wstB = fv("wstB", 128, NDC * D)
    wqbf = fv("wqbf", 128, NDC * D, BF16)
    wkbf = fv("wkbf", 128, NDC * D, BF16)
    wvbf = fv("wvbf", 128, NDC * D, BF16)

    def _(sync):
        for dst, src in ((ident, ident_d), (ind16, ind16_d),
                         (bqpp, bqpp_d), (bkpp, bkpp_d), (bopp, bopp_d),
                         (bvrow, bvrow_d)):
            dma(sync, dst, src, "dl")
        for dc in range(NDC):
            dma(sync, xs[:, dc * T:(dc + 1) * T], xT[dc * 128:(dc + 1) * 128, :], "dl")
    SYN(_)
    LD_X = C["dl"]

    def _(v):
        v.memset(ones_r, 1.0)
        v.memset(ones_c, 1.0)
        v.memset(cst[0:1, 0:1], 127.0)
        v.memset(cst[0:1, 1:2], 32767.0)
        v.memset(cst[0:1, 2:3], 7.0)
        v.memset(eps_t[0:1, 0:1], 1e-5)
        v.memset(ccst, 0.0)
        v.drain()
        v.reciprocal(sl("rc127"), cst[0:1, 0:1])
        v.reciprocal(sl("rc32767"), cst[0:1, 1:2])
        v.reciprocal(sl("rc7"), cst[0:1, 2:3])
        W(v, "dl", LD_X)
        for dc in range(NDC):
            red512(v, msl[:, dc:dc + 1], xs[:, dc * T:(dc + 1) * T], T)
        I(v.drain(), "vv")
    VEC(_)
    crossmax(0, NDC, "mx_x")

    def _(v):
        W(v, "vv")
        v.tensor_copy(ccst[0:1, 0:1], sl("mx_x"))
        I(v.drain(), "vv")
    VEC(_)
    CC1 = collective()

    wslot = {"q": wstA, "k": wstB, "v": wstA}
    wbfs = {"q": wqbf, "k": wkbf, "v": wvbf}
    WQUANT_DONE = {}
    names3 = ("q", "k", "v")
    for wi, nm in enumerate(names3):
        st = wslot[nm]

        def _(sync, nm=nm, st=st, wi=wi):
            if wi >= 2:
                W(sync, "vv", WQUANT_DONE[names3[wi - 2]])
            for dc in range(NDC):
                dma(sync, st[:, dc * D:(dc + 1) * D],
                    wTd[nm][dc * 128:(dc + 1) * 128, :], "dl")
        SYN(_)
        ld = C["dl"]

        def _(v, nm=nm, st=st, ld=ld):
            W(v, "dl", ld)
            for dc in range(NDC):
                red512(v, msl[:, 8 + dc:9 + dc], st[:, dc * D:(dc + 1) * D], D)
            I(v.drain(), "vv")
        VEC(_)
        crossmax(8, NDC, "m_w" + nm)

        def _(v, nm=nm):
            W(v, "vv")
            v.tensor_scalar(sl("s_w" + nm), sl("m_w" + nm), sl("rc127"), 1e-8,
                            OP.mult, op1=OP.max)
            v.drain()
            v.reciprocal(sl("r_w" + nm), sl("s_w" + nm))
            I(v.drain(), "vv")
        VEC(_)
        bc_round(["r_w" + nm])

        def _(v, nm=nm, st=st):
            W(v, "vv")
            for dc in range(NDC):
                for j in range(0, D, 512):
                    v.tensor_scalar(qtmp[:, j:j + 512],
                                    st[:, dc * D + j:dc * D + j + 512],
                                    bca("r_w" + nm), MG, OP.mult, op1=OP.add)
                    v.tensor_scalar(wbfs[nm][:, dc * D + j:dc * D + j + 512],
                                    qtmp[:, j:j + 512], -MG, None, OP.add)
            I(v.drain(), "vv")
        VEC(_)
        WQUANT_DONE[nm] = C["vv"]

    def _(v):
        W(v, "cc", CC1)
        v.tensor_copy(sl("gm_x"), ccrd[0:1, 0:1])
        v.drain()
        v.tensor_scalar(sl("s_x"), sl("gm_x"), sl("rc127"), 1e-8, OP.mult, op1=OP.max)
        v.drain()
        v.reciprocal(sl("r_x"), sl("s_x"))
        v.tensor_tensor(sl("sxw_q"), sl("s_x"), sl("s_wq"), OP.mult)
        v.tensor_tensor(sl("sxw_k"), sl("s_x"), sl("s_wk"), OP.mult)
        v.tensor_tensor(sl("sxw_v"), sl("s_x"), sl("s_wv"), OP.mult)
        I(v.drain(), "vv")
    VEC(_)
    bc_round(["r_x", "sxw_q", "sxw_k", "sxw_v"])

    A.free("wstA", "wstB")
    xq = fv("xq", 128, NDC * T, BF16)

    def _(v):
        W(v, "vv")
        for dc in range(NDC):
            for j in range(0, T, 512):
                v.tensor_scalar(qtmp[:, j:j + 512],
                                xs[:, dc * T + j:dc * T + j + 512], bca("r_x"), MG,
                                OP.mult, op1=OP.add)
                v.tensor_scalar(xq[:, dc * T + j:dc * T + j + 512],
                                qtmp[:, j:j + 512], -MG, None, OP.add)
        I(v.drain(), "vv")
    VEC(_)
    A.free("xs", "xs_pad")

    F1F = fence()
    v65 = fv("v65", 128, 16 * NH * 65, BF16)
    v65r = v65.rearrange("p (t h c) -> p t h c", t=16, h=NH, c=65)
    qbf = fv("qbf", 128, NDC * T, BF16)
    kbf = fv("kbf", 128, NDC * T, BF16)
    sqt = fv("sqt", 128, 2 * 512, BF16)

    with nc.psum_tensor("psA1", [128, 512], F32) as psA, \
         nc.psum_tensor("psB1", [128, 512], F32) as psB, \
         nc.psum_tensor("psN1", [16, 2048], F32) as psN:
        psAB = (psA, psB)

        def _(t):
            W(t, "dl", LD_X)
            t.matmul(psA[:, :], ones_r, bvrow[0:1, 0:512], start=True, stop=True)
            I(t.matmul(psB[:, :], ones_r, bvrow[0:1, 512:1024], start=True, stop=True),
              "pe")
        TEN(_)
        s1 = C["pe"]

        def _(v):
            W(v, "pe", s1)
            v.tensor_copy(bvbc[:, 0:512], psA[:, :])
            v.tensor_copy(bvbc[:, 512:1024], psB[:, :])
            v.memset(v65r[:, :, :, 64:65], 1.0)
            PR(v)
            I(v.drain(), "vv")
        VEC(_)
        evg = [C["vv"], C["vv"]]

        for nm, obf, bpp in (("q", qbf, bqpp), ("k", kbf, bkpp)):
            for oc in range(NDC):
                for tt in range(NTT):
                    par = (oc * NTT + tt) % 2
                    ps = psAB[par]

                    def _(t, ps=ps, par=par, nm=nm, oc=oc, tt=tt):
                        W(t, "vv", evg[par])
                        for dc in range(NDC):
                            inst = t.matmul(
                                ps[:, :],
                                wbfs[nm][:, dc * D + oc * 128:dc * D + oc * 128 + 128],
                                xq[:, dc * T + tt * 512:dc * T + tt * 512 + 512],
                                start=(dc == 0), stop=(dc == NDC - 1))
                        I(inst, "pe")
                    TEN(_)
                    pes = C["pe"]

                    def _(v, ps=ps, obf=obf, nm=nm, bpp=bpp, oc=oc, tt=tt, pes=pes):
                        W(v, "pe", pes)
                        v.tensor_scalar(
                            obf[:, oc * T + tt * 512:oc * T + tt * 512 + 512],
                            ps[:, :], bca("sxw_" + nm), bpp[:, oc:oc + 1],
                            OP.mult, op1=OP.add)
                        PR(v)
                        I(v.drain(), "vv")
                    VEC(_)
                    evg[par] = C["vv"]

        for vt in range(16):
            for c2 in range(2):
                par = (vt * 2 + c2) % 2
                ps = psAB[par]

                def _(t, ps=ps, par=par, vt=vt, c2=c2):
                    W(t, "vv", evg[par])
                    for dc in range(NDC):
                        inst = t.matmul(
                            ps[:, :],
                            xq[:, dc * T + vt * 128:dc * T + vt * 128 + 128],
                            wvbf[:, dc * D + c2 * 512:dc * D + c2 * 512 + 512],
                            start=(dc == 0), stop=(dc == NDC - 1))
                    I(inst, "pe")
                TEN(_)
                pes = C["pe"]

                def _(v, ps=ps, vt=vt, c2=c2, pes=pes):
                    W(v, "pe", pes)
                    v.scalar_tensor_tensor(v65r[:, vt, 8 * c2:8 * c2 + 8, 0:64],
                                           ps[:, :], bca("sxw_v"),
                                           bvbc[:, c2 * 512:c2 * 512 + 512],
                                           OP.mult, OP.add)
                    PR(v)
                    I(v.drain(), "vv")
                VEC(_)
                evg[par] = C["vv"]

        QKV_DONE = C["vv"]

        psn_free = C["vv"]
        for si, obf in enumerate((qbf, kbf)):
            sqg = [C["pe"], C["pe"]]
            for tt in range(NTT):
                for dc in range(NDC):
                    par = dc % 2
                    sq = sqt[:, par * 512:par * 512 + 512]

                    def _(s, sq=sq, obf=obf, dc=dc, tt=tt, g=sqg[par]):
                        W(s, "pe", g)
                        if dc == 0 and tt == 0:
                            W(s, "vv", QKV_DONE)
                        s.activation(sq,
                                     obf[:, dc * T + tt * 512:dc * T + tt * 512 + 512],
                                     AF.Square)
                        I(s.drain(), "ss")
                    SCA(_)
                    sss = C["ss"]

                    def _(t, sq=sq, dc=dc, tt=tt, sss=sss, pf=psn_free):
                        W(t, "ss", sss)
                        if dc == 0 and tt == 0:
                            W(t, "vv", pf)
                        I(t.matmul(psN[:, tt * 512:tt * 512 + 512],
                                   ind16[:, dc * 16:dc * 16 + 16], sq,
                                   start=(dc == 0), stop=(dc == NDC - 1)), "pe")
                    TEN(_)
                    sqg[par] = C["pe"]
            pes = C["pe"]

            def _(v, si=si, pes=pes):
                W(v, "pe", pes)
                for j in range(4):
                    v.tensor_reduce(msl[0:16, 24 + j:25 + j],
                                    psN[:, j * 512:(j + 1) * 512], AX.X, OP.max)
                v.drain()
                v.tensor_reduce(nmcol[:, si:si + 1], msl[0:16, 24:28], AX.X, OP.max)
                PR(v)
                I(v.drain(), "vv")
            VEC(_)
            psn_free = C["vv"]

        def _(s):
            W(s, "vv")
            s.activation(nmcol[:, 2:3], nmcol[:, 0:1], AF.Sqrt)
            s.activation(nmcol[:, 3:4], nmcol[:, 1:2], AF.Sqrt)
            I(s.drain(), "ss")
        SCA(_)
        s1 = C["ss"]

        def _(v, s1=s1):
            W(v, "ss", s1)
            v.tensor_tensor(nmcol[:, 4:5], nmcol[:, 2:3], nmcol[:, 3:4], OP.mult)
            v.drain()
            v.tensor_scalar(nmcol[:, 5:6], nmcol[:, 4:5], -0.1275, -0.001,
                            OP.mult, op1=OP.add)
            I(v.drain(), "vv")
        VEC(_)
        s1 = C["vv"]

        def _(t, s1=s1):
            W(t, "vv", s1)
            I(t.transpose(ps_bc[0:1, 0:16], nmcol[0:16, 5:6], ident[0:16, 0:16]), "pe")
        TEN(_)
        s1 = C["pe"]

        def _(v, s1=s1):
            W(v, "pe", s1)
            v.tensor_copy(nmrow, ps_bc[0:1, 0:16])
            PR(v)
            I(v.drain(), "vv")
        VEC(_)
        s1 = C["vv"]

        def _(t, s1=s1):
            W(t, "vv", s1)
            I(t.matmul(ps_bc[:, 0:16], ones_r, nmrow, start=True, stop=True), "pe")
        TEN(_)
        s1 = C["pe"]

        def _(v, s1=s1):
            W(v, "pe", s1)
            v.tensor_copy(negM, ps_bc[:, 0:16])
            PR(v)
            I(v.drain(), "vv")
        VEC(_)
        NEGM_DONE = C["vv"]

    A.free("xq", "wqbf", "wkbf", "wvbf", "sqt", "bvrow", "bvbc")

    _pb1 = fv("Pb1", 128, 16 * 512, BF16)
    Pbuf = [fv("Pb0", 128, 12 * 512, BF16), _pb1]
    ystg = fv("ystg", 128, 2 * 512)
    rs_row = fv("rs_row", 1, 2 * 512)

    with nc.psum_tensor("psS0", [128, 512], F32) as psS0, \
         nc.psum_tensor("psS1", [128, 512], F32) as psS1, \
         nc.psum_tensor("psY0", [65, 512], F32) as psY0, \
         nc.psum_tensor("psY1", [65, 512], F32) as psY1, \
         nc.psum_tensor("psR0", [64, 512], F32) as psR0, \
         nc.psum_tensor("psR1", [64, 512], F32) as psR1:
        psS, psY, psR = (psS0, psS1), (psY0, psY1), (psR0, psR1)
        eg = [C["ss"], C["ss"]]
        pbg = [NEGM_DONE, NEGM_DONE]
        yg = [NEGM_DONE, NEGM_DONE]
        ystg_dg = [C["dg"], C["dg"]]

        def emit_scores(h, qt, pp):
            base, dch, ktn = 64 * (h % 2), h // 2, 4 * qt + 4
            for kt in range(ktn):
                par = kt % 2

                def _(t, h=h, qt=qt, kt=kt, par=par, base=base, dch=dch, pp=pp):
                    W(t, "ss", eg[par])
                    if kt == 0:
                        W(t, "vv", pbg[pp])
                    I(t.matmul(
                        psS[par][:, :],
                        kbf[base:base + 64, dch * T + kt * 128:dch * T + kt * 128 + 128],
                        qbf[base:base + 64, dch * T + qt * 512:dch * T + qt * 512 + 512],
                        start=True, stop=True), "pe")
                TEN(_)
                pes = C["pe"]

                def _(s, h=h, kt=kt, par=par, pp=pp, pes=pes):
                    W(s, "pe", pes)
                    if kt == 0:
                        W(s, "vv", NEGM_DONE)
                    s.activation(Pbuf[pp][:, kt * 512:kt * 512 + 512], psS[par][:, :],
                                 AF.Exp, bias=negM[:, h:h + 1], scale=0.125)
                    PR(s)
                    I(s.drain(), "ss")
                SCA(_)
                eg[par] = C["ss"]
            ssa = C["ss"]

            def _(g, qt=qt, pp=pp, ssa=ssa):
                W(g, "ss", ssa)
                for kt in range(4 * qt, 4 * qt + 4):
                    off = 128 * (kt - 4 * qt)
                    if off > 0:
                        g.memset(Pbuf[pp][:, kt * 512:kt * 512 + off], 0.0)
                    g.affine_select(Pbuf[pp][:, kt * 512 + off:kt * 512 + off + 128],
                                    Pbuf[pp][:, kt * 512 + off:kt * 512 + off + 128],
                                    [[1, 128]], OP.is_ge, 0.0,
                                    base=0, channel_multiplier=-1)
                I(g.drain(), "gg")
            GPS(_)
            return ssa, C["gg"]

        def emit_pv(h, qt, pp, rss, rgg):
            base, dch, ktn = 64 * (h % 2), h // 2, 4 * qt + 4
            par = (h * NTT + qt) % 2

            def _(t, h=h, pp=pp, par=par, ktn=ktn, rss=rss, rgg=rgg):
                W(t, "ss", rss)
                W(t, "gg", rgg)
                W(t, "vv", yg[par])
                for kt in range(ktn):
                    inst = t.matmul(psY[par][:, :], v65r[:, kt, h, :],
                                    Pbuf[pp][:, kt * 512:kt * 512 + 512],
                                    start=(kt == 0), stop=(kt == ktn - 1))
                I(inst, "pe")
            TEN(_)
            pes = C["pe"]
            rsl = rs_row[0:1, par * 512:par * 512 + 512]

            def _(v, par=par, pes=pes, rsl=rsl):
                W(v, "pe", pes)
                v.reciprocal(rsl, psY[par][64:65, :])
                PR(v)
                I(v.drain(), "vv")
            VEC(_)
            s1 = C["vv"]

            def _(t, par=par, s1=s1, rsl=rsl):
                W(t, "vv", s1)
                I(t.matmul(psR[par][:, :], ones_r[0:1, 0:64], rsl,
                           start=True, stop=True), "pe")
            TEN(_)
            pes2 = C["pe"]

            def _(v, par=par, pes2=pes2, g=ystg_dg[par]):
                W(v, "pe", pes2)
                W(v, "dg", g)
                ysl = ystg[0:64, par * 512:par * 512 + 512]
                v.tensor_copy(ysl, psY[par][0:64, :])
                v.tensor_tensor(ysl, ysl, psR[par][:, :], OP.mult)
                PR(v)
                I(v.drain(), "vv")
            VEC(_)
            yg[par] = C["vv"]
            s2 = C["vv"]

            def _(g, h=h, qt=qt, par=par, s2=s2):
                W(g, "vv", s2)
                dma(g, y_hp[64 * h:64 * h + 64, qt * 512:qt * 512 + 512],
                    ystg[0:64, par * 512:par * 512 + 512], "dg")
            GPS(_)
            ystg_dg[par] = C["dg"]
            return C["vv"]

        prev = None
        for h in range(NH):
            for qt in range(NTT):
                pp = qt % 2
                rss, rgg = emit_scores(h, qt, pp)
                if prev is not None:
                    pbg[prev[2]] = emit_pv(*prev)
                prev = (h, qt, pp, rss, rgg)
        emit_pv(*prev)
        Y_DONE = C["vv"]
        Y_DG = C["dg"]

    A.free("Pb0", "Pb1", "qbf", "kbf", "v65", "ystg")

    F3F = fence()
    yst = fv("yst", 128, 2 * T)

    ylg = [C["vv"], C["vv"]]
    for dc in range(NDC):
        par = dc % 2

        def _(sync, dc=dc, par=par, g=ylg[par]):
            W(sync, "vv", g)
            if dc < 2:
                wfence(sync, F3F)
                W(sync, "dg", Y_DG)
            dma(sync, yst[:, par * T:(par + 1) * T], y_hp[dc * 128:(dc + 1) * 128, :],
                "dl")
        SYN(_)
        ld = C["dl"]

        def _(v, dc=dc, par=par, ld=ld):
            W(v, "dl", ld)
            red512(v, msl[:, dc:dc + 1], yst[:, par * T:(par + 1) * T], T)
            I(v.drain(), "vv")
        VEC(_)
        ylg[par] = C["vv"]
    crossmax(0, NDC, "m_y")

    def _(v):
        W(v, "vv")
        v.tensor_copy(ccst[0:1, 0:1], sl("m_y"))
        I(v.drain(), "vv")
    VEC(_)
    CC2 = collective()

    wst3 = fv("wst3", 128, NDC * D)
    wobf = fv("wobf", 128, NDC * D, BF16)

    def _(sync):
        wfence(sync, F3F)
        for dc in range(NDC):
            dma(sync, wst3[:, dc * D:(dc + 1) * D], wTd["o"][dc * 128:(dc + 1) * 128, :],
                "dl")
    SYN(_)
    ld = C["dl"]

    def _(v, ld=ld):
        W(v, "dl", ld)
        for dc in range(NDC):
            red512(v, msl[:, 8 + dc:9 + dc], wst3[:, dc * D:(dc + 1) * D], D)
        I(v.drain(), "vv")
    VEC(_)
    crossmax(8, NDC, "m_wo")

    def _(v):
        W(v, "vv")
        v.tensor_scalar(sl("s_wo"), sl("m_wo"), sl("rc127"), 1e-8, OP.mult, op1=OP.max)
        v.drain()
        v.reciprocal(sl("r_wo"), sl("s_wo"))
        I(v.drain(), "vv")
    VEC(_)
    bc_round(["r_wo"])

    def _(v):
        W(v, "vv")
        for dc in range(NDC):
            for j in range(0, D, 512):
                v.tensor_scalar(qtmp[:, j:j + 512],
                                wst3[:, dc * D + j:dc * D + j + 512],
                                bca("r_wo"), MG, OP.mult, op1=OP.add)
                v.tensor_scalar(wobf[:, dc * D + j:dc * D + j + 512],
                                qtmp[:, j:j + 512], -MG, None, OP.add)
        I(v.drain(), "vv")
    VEC(_)
    A.free("wst3")

    def _(v):
        W(v, "cc", CC2)
        v.tensor_copy(sl("gm_y"), ccrd[0:1, 0:1])
        v.drain()
        v.tensor_scalar(sl("s_y"), sl("gm_y"), sl("rc127"), 1e-8, OP.mult, op1=OP.max)
        v.drain()
        v.reciprocal(sl("r_y"), sl("s_y"))
        v.tensor_tensor(sl("syw_o"), sl("s_y"), sl("s_wo"), OP.mult)
        I(v.drain(), "vv")
    VEC(_)
    bc_round(["r_y", "syw_o"])

    yq = fv("yq", 128, NDC * T, BF16)
    lnr = fv("lnr", 1, 6 * 512)

    yqg = [C["vv"], C["vv"]]
    for dc in range(NDC):
        par = dc % 2

        def _(sync, dc=dc, par=par, g=yqg[par]):
            W(sync, "vv", g)
            dma(sync, yst[:, par * T:(par + 1) * T], y_hp[dc * 128:(dc + 1) * 128, :],
                "dl")
        SYN(_)
        ld = C["dl"]

        def _(v, dc=dc, par=par, ld=ld):
            W(v, "dl", ld)
            W(v, "vv")
            for j in range(0, T, 512):
                v.tensor_scalar(qtmp[:, j:j + 512],
                                yst[:, par * T + j:par * T + j + 512], bca("r_y"),
                                MG, OP.mult, op1=OP.add)
                v.tensor_scalar(yq[:, dc * T + j:dc * T + j + 512],
                                qtmp[:, j:j + 512], -MG, None, OP.add)
            I(v.drain(), "vv")
        VEC(_)
        yqg[par] = C["vv"]
    YQ_DONE = C["vv"]
    A.free("yst")

    xs2 = fv("xs2", 128, NDC * T)
    xrr = fv("xrr", 128, 2 * 512)
    t1f = fv("t1f", 128, 512)
    sqf = fv("sqf", 128, 2 * 512)

    def emit_ln(xbuf, col_of, tts, psM, psQ, psU, psV, ready_vv):
        mean = lnr[0:1, 0:512]
        msq = lnr[0:1, 512:1024]
        arow = lnr[0:1, 1024:1536]
        brow = lnr[0:1, 1536:2048]
        tmp = lnr[0:1, 2048:2560]
        for tt in tts:
            sqg = [C["pe"], C["pe"]]
            for oc in range(NDC):
                par = oc % 2
                sq = sqf[:, par * 512:par * 512 + 512]

                def _(t, oc=oc, tt=tt, ready_vv=ready_vv):
                    if oc == 0:
                        W(t, "vv", ready_vv)
                    I(t.matmul(psM[:, :], ones_c,
                               xbuf[:, col_of(oc, tt):col_of(oc, tt) + 512],
                               start=(oc == 0), stop=(oc == NDC - 1)), "pe")
                TEN(_)

                def _(v, sq=sq, oc=oc, tt=tt, g=sqg[par], ready_vv=ready_vv):
                    W(v, "pe", g)
                    v.tensor_tensor(sq, xbuf[:, col_of(oc, tt):col_of(oc, tt) + 512],
                                    xbuf[:, col_of(oc, tt):col_of(oc, tt) + 512],
                                    OP.mult)
                    I(v.drain(), "vv")
                VEC(_)
                s1 = C["vv"]

                def _(t, sq=sq, oc=oc, s1=s1):
                    W(t, "vv", s1)
                    I(t.matmul(psQ[:, :], ones_c, sq,
                               start=(oc == 0), stop=(oc == NDC - 1)), "pe")
                TEN(_)
                sqg[par] = C["pe"]
            pes = C["pe"]

            def _(v, pes=pes):
                W(v, "pe", pes)
                v.tensor_scalar(mean, psM[:, :], 1.0 / 1024.0, None, OP.mult)
                v.tensor_scalar(msq, psQ[:, :], 1.0 / 1024.0, None, OP.mult)
                v.tensor_tensor(tmp, mean, mean, OP.mult)
                v.tensor_tensor(msq, msq, tmp, OP.subtract)
                PR(v)
                I(v.drain(), "vv")
            VEC(_)
            s1 = C["vv"]

            def _(s, s1=s1):
                W(s, "vv", s1)
                s.activation(tmp, msq, AF.Ln, bias=eps_t[0:1, 0:1])
                s.activation(arow, tmp, AF.Exp, scale=-0.5)
                I(s.drain(), "ss")
            SCA(_)
            s1 = C["ss"]

            def _(v, s1=s1):
                W(v, "ss", s1)
                v.scalar_tensor_tensor(brow, mean, -1.0, arow, OP.mult, OP.mult)
                I(v.drain(), "vv")
            VEC(_)
            s1 = C["vv"]

            def _(t, s1=s1):
                W(t, "vv", s1)
                t.matmul(psU[:, :], ones_r, arow, start=True, stop=True)
                I(t.matmul(psV[:, :], ones_r, brow, start=True, stop=True), "pe")
            TEN(_)
            pes = C["pe"]

            def _(v, tt=tt, pes=pes):
                W(v, "pe", pes)
                for oc in range(NDC):
                    csl = xbuf[:, col_of(oc, tt):col_of(oc, tt) + 512]
                    v.tensor_tensor(t1f, csl, psU[:, :], OP.mult)
                    v.tensor_tensor(csl, t1f, psV[:, :], OP.add)
                PR(v)
                I(v.drain(), "vv")
            VEC(_)

    with nc.psum_tensor("psA3", [128, 512], F32) as psA, \
         nc.psum_tensor("psB3", [128, 512], F32) as psB, \
         nc.psum_tensor("psM3", [1, 512], F32) as psM, \
         nc.psum_tensor("psQ3", [1, 512], F32) as psQ, \
         nc.psum_tensor("psU3", [128, 512], F32) as psU, \
         nc.psum_tensor("psV3", [128, 512], F32) as psV:
        psAB = (psA, psB)
        evg = [YQ_DONE, YQ_DONE]
        xrg = [YQ_DONE, YQ_DONE]
        for oc in range(NDC):
            for tt in range(NTT):
                par = (oc * NTT + tt) % 2
                ps = psAB[par]

                def _(sync, oc=oc, tt=tt, par=par):
                    W(sync, "vv", xrg[par])
                    dma(sync, xrr[:, par * 512:par * 512 + 512],
                        xT[oc * 128:oc * 128 + 128, tt * 512:tt * 512 + 512], "dl")
                SYN(_)
                ld = C["dl"]

                def _(t, ps=ps, par=par, oc=oc, tt=tt):
                    W(t, "vv", evg[par])
                    for dc in range(NDC):
                        inst = t.matmul(
                            ps[:, :],
                            wobf[:, dc * D + oc * 128:dc * D + oc * 128 + 128],
                            yq[:, dc * T + tt * 512:dc * T + tt * 512 + 512],
                            start=(dc == 0), stop=(dc == NDC - 1))
                    I(inst, "pe")
                TEN(_)
                pes = C["pe"]

                def _(v, ps=ps, par=par, oc=oc, tt=tt, pes=pes, ld=ld):
                    W(v, "pe", pes)
                    W(v, "dl", ld)
                    v.tensor_scalar(t1f, ps[:, :], bca("syw_o"), bopp[:, oc:oc + 1],
                                    OP.mult, op1=OP.add)
                    v.tensor_tensor(xs2[:, oc * T + tt * 512:oc * T + tt * 512 + 512],
                                    t1f, xrr[:, par * 512:par * 512 + 512], OP.add)
                    PR(v)
                    I(v.drain(), "vv")
                VEC(_)
                evg[par] = C["vv"]
                xrg[par] = C["vv"]
        OPROJ_DONE = C["vv"]

        emit_ln(xs2, lambda oc, tt: oc * T + tt * 512, range(NTT),
                psM, psQ, psU, psV, OPROJ_DONE)
        LN1_DONE = C["vv"]

    def _(v):
        W(v, "vv", LN1_DONE)
        for dc in range(NDC):
            red512(v, msl[:, dc:dc + 1], xs2[:, dc * T:(dc + 1) * T], T)
        I(v.drain(), "vv")
    VEC(_)
    crossmax(0, NDC, "m_x2")

    def _(v):
        W(v, "vv")
        v.tensor_copy(ccst[0:1, 0:1], sl("m_x2"))
        I(v.drain(), "vv")
    VEC(_)
    CC3 = collective()

    def _(v):
        W(v, "cc", CC3)
        v.tensor_copy(sl("gm_x2"), ccrd[0:1, 0:1])
        v.drain()
        v.tensor_scalar(sl("s_x2"), sl("gm_x2"), sl("rc32767"), 1e-12, OP.mult,
                        op1=OP.max)
        v.drain()
        v.reciprocal(sl("r_x2"), sl("s_x2"))
        v.tensor_copy(sl("rc7b"), sl("rc7"))
        I(v.drain(), "vv")
    VEC(_)
    bc_round(["r_x2", "s_x2", "rc7b"])

    A.free("yq", "wobf", "xrr", "t1f", "sqf", "lnr")
    x2q = fv("x2q", 128, NDC * T, BF16)

    def _(v):
        W(v, "vv")
        for dc in range(NDC):
            for j in range(0, T, 512):
                v.tensor_scalar(qtmp[:, j:j + 512],
                                xs2[:, dc * T + j:dc * T + j + 512], bca("r_x2"),
                                MG, OP.mult, op1=OP.add)
                v.tensor_scalar(x2q[:, dc * T + j:dc * T + j + 512],
                                qtmp[:, j:j + 512], -MG, None, OP.add)
        I(v.drain(), "vv")
    VEC(_)
    X2Q_DONE = C["vv"]

    def _(g):
        W(g, "vv", X2Q_DONE)
        for dc in range(NDC):
            dma(g, x2sp[dc * 128:(dc + 1) * 128, :], xs2[:, dc * T:(dc + 1) * T], "dg")
    GPS(_)
    X2SPILL = C["dg"]
    A.free("xs2")

    F4F = fence()
    indk = fv("indk", 96, NIC * 128)

    def _(sync):
        wfence(sync, F4F)
        dma(sync, indk, indk_d, "dl")
    SYN(_)

    def prep_ffn_weight(nm, w_native, w_trans, wbf, nblk, n_ot, n_ic, prev_done):
        O = n_ot * 128
        nst = fv("nst" + nm, 128, nblk * 32)
        tst = fv("tst" + nm, 128, O)
        sT = fv("sT" + nm, nblk, O)
        rT = fv("rT" + nm, nblk, O)
        bm = fv("bm" + nm, 128, nblk)
        ssc = fv("ssc" + nm, 128, nblk)
        rsc = fv("rsc" + nm, 128, nblk)
        I1 = nblk * 32
        with nc.psum_tensor("psTs" + nm, [nblk, 128], F32) as psTs, \
             nc.psum_tensor("psTr" + nm, [nblk, 128], F32) as psTr, \
             nc.psum_tensor("psRB" + nm, [128, 512], F32) as psRB, \
             nc.psum_tensor("psSB" + nm, [128, 512], F32) as psSB:
            nstg = [prev_done]
            tcopy_vv = C["vv"]
            trps = [C["pe"]]
            for ot in range(n_ot):
                def _(sync, ot=ot):
                    W(sync, "vv", nstg[0])
                    if ot < 2:
                        wfence(sync, F4F)
                    dma(sync, nst, w_native[ot * 128:(ot + 1) * 128, :], "dl")
                SYN(_)
                ld = C["dl"]

                def _(v, ld=ld, nblk=nblk):
                    W(v, "dl", ld)
                    W(v, "pe", trps[0])
                    r3 = nst.rearrange("p (b i) -> p b i", i=32)
                    for j in range(0, nblk, 16):
                        v.tensor_reduce(bm[:, j:j + 16], r3[:, j:j + 16, :],
                                        AX.X, OP.max, apply_absolute_value=True)
                    v.drain()
                    v.tensor_scalar(ssc[:, :], bm[:, :], bca("rc7b"), 1e-12,
                                    OP.mult, op1=OP.max)
                    v.drain()
                    v.reciprocal(rsc[:, :], ssc[:, :])
                    I(v.drain(), "vv")
                VEC(_)
                nstg[0] = C["vv"]
                s1 = C["vv"]

                def _(t, s1=s1, g=tcopy_vv):
                    W(t, "vv", max(s1, g))
                    t.transpose(psTs[:, :], ssc[:, :], ident)
                    I(t.transpose(psTr[:, :], rsc[:, :], ident), "pe")
                TEN(_)
                pes = C["pe"]
                trps[0] = pes

                def _(v, ot=ot, pes=pes):
                    W(v, "pe", pes)
                    v.tensor_copy(sT[:, ot * 128:(ot + 1) * 128], psTs[:, :])
                    v.tensor_copy(rT[:, ot * 128:(ot + 1) * 128], psTr[:, :])
                    PR(v)
                    I(v.drain(), "vv")
                VEC(_)
                tcopy_vv = C["vv"]
            SCALES_DONE = C["vv"]

            tstg = [SCALES_DONE]
            bcg = [C["vv"]]
            for ic in range(n_ic):
                def _(sync, ic=ic):
                    W(sync, "vv", tstg[0])
                    dma(sync, tst, w_trans[ic * 128:(ic + 1) * 128, :], "dl")
                SYN(_)
                ld = C["dl"]
                for os_ in range(O // 512):
                    def _(t, ic=ic, os_=os_, sd=SCALES_DONE, g=bcg[0]):
                        W(t, "vv", max(sd, g))
                        t.matmul(psRB[:, :], indk[0:nblk, ic * 128:(ic + 1) * 128],
                                 rT[:, os_ * 512:os_ * 512 + 512],
                                 start=True, stop=True)
                        I(t.matmul(psSB[:, :], indk[0:nblk, ic * 128:(ic + 1) * 128],
                                   sT[:, os_ * 512:os_ * 512 + 512],
                                   start=True, stop=True), "pe")
                    TEN(_)
                    pes = C["pe"]

                    def _(v, ic=ic, os_=os_, pes=pes, ld=ld, wbf=wbf, O=O):
                        W(v, "pe", pes)
                        W(v, "dl", ld)
                        v.tensor_tensor(qtmp[:, 0:512],
                                        tst[:, os_ * 512:os_ * 512 + 512],
                                        psRB[:, :], OP.mult)
                        v.tensor_scalar(qtmp[:, 0:512], qtmp[:, 0:512], MG, -MG,
                                        OP.add, op1=OP.add)
                        v.tensor_tensor(wbf[:, ic * O + os_ * 512:ic * O + os_ * 512 + 512],
                                        qtmp[:, 0:512], psSB[:, :], OP.mult)
                        PR(v)
                        I(v.drain(), "vv")
                    VEC(_)
                    bcg[0] = C["vv"]
                tstg[0] = C["vv"]
        A.free("nst" + nm, "tst" + nm, "sT" + nm, "rT" + nm,
               "bm" + nm, "ssc" + nm, "rsc" + nm)
        return C["vv"]

    wubf = fv("wubf", 128, NDC * DI, BF16)
    wgbf = fv("wgbf", 128, NDC * DI, BF16)
    dn1 = prep_ffn_weight("u", wu_n, wuT, wubf, 32, NIC, NDC, C["vv"])
    dn2 = prep_ffn_weight("g", wg_n, wgT, wgbf, 32, NIC, NDC, dn1)
    WU_DONE = C["vv"]

    upst = fv("upst", 128, 2 * T, BF16)
    gtst = fv("gtst", 128, 2 * T, BF16)

    with nc.psum_tensor("psUA", [128, 512], F32) as psUA, \
         nc.psum_tensor("psUB", [128, 512], F32) as psUB, \
         nc.psum_tensor("psGA", [128, 512], F32) as psGA, \
         nc.psum_tensor("psGB", [128, 512], F32) as psGB:
        psU2 = (psUA, psUB)
        psG2 = (psGA, psGB)
        evgU = [C["ss"], C["ss"]]
        evgG = [C["ss"], C["ss"]]
        stage_vv = {("u", 0): 0, ("u", 1): 0, ("g", 0): 0, ("g", 1): 0}
        stage_dg = dict(stage_vv)
        first_mm = True
        first_evac = True
        for oc in range(NIC):
            spar = oc % 2
            for tensor_nm, psP, evg, wbf, stage, hbuf in (
                    ("u", psU2, evgU, wubf, upst, up_hp),
                    ("g", psG2, evgG, wgbf, gtst, gt_hp)):
                for tt in range(NTT):
                    par = tt % 2
                    ps = psP[par]

                    def _(t, ps=ps, wbf=wbf, oc=oc, tt=tt, g=evg[par], fm=first_mm):
                        W(t, "ss", g)
                        if fm:
                            W(t, "vv", WU_DONE)
                        for ic in range(NDC):
                            inst = t.matmul(
                                ps[:, :],
                                wbf[:, ic * DI + oc * 128:ic * DI + oc * 128 + 128],
                                x2q[:, ic * T + tt * 512:ic * T + tt * 512 + 512],
                                start=(ic == 0), stop=(ic == NDC - 1))
                        I(inst, "pe")
                    TEN(_)
                    first_mm = False
                    pes = C["pe"]

                    def _(s, ps=ps, stage=stage, spar=spar, tt=tt, pes=pes,
                          fe=first_evac, gvv=stage_vv[(tensor_nm, spar)],
                          gdg=stage_dg[(tensor_nm, spar)]):
                        W(s, "pe", pes)
                        if fe:
                            W(s, "dg", X2SPILL)
                        if tt == 0:
                            s.wait_ge(SEMS["vv"], gvv)
                            s.wait_ge(SEMS["dg"], gdg)
                        s.mul(stage[:, spar * T + tt * 512:spar * T + tt * 512 + 512],
                              ps[:, :], bca("s_x2"))
                        PR(s)
                        I(s.drain(), "ss")
                    SCA(_)
                    first_evac = False
                    evg[par] = C["ss"]
                ssa = C["ss"]
                slot = 0 if tensor_nm == "u" else 2

                def _(v, stage=stage, spar=spar, oc=oc, slot=slot, ssa=ssa):
                    W(v, "ss", ssa)
                    red512(v, msl[:, slot + 1:slot + 2],
                           stage[:, spar * T:(spar + 1) * T], T)
                    if oc == 0:
                        v.tensor_copy(msl[:, slot:slot + 1], msl[:, slot + 1:slot + 2])
                    else:
                        v.tensor_tensor(msl[:, slot:slot + 1], msl[:, slot:slot + 1],
                                        msl[:, slot + 1:slot + 2], OP.max)
                    I(v.drain(), "vv")
                VEC(_)

                def _(g, stage=stage, spar=spar, oc=oc, hbuf=hbuf, ssa=ssa):
                    W(g, "ss", ssa)
                    dma(g, hbuf[oc * 128:(oc + 1) * 128, :],
                        stage[:, spar * T:(spar + 1) * T], "dg")
                GPS(_)
                stage_vv[(tensor_nm, spar)] = C["vv"]
                stage_dg[(tensor_nm, spar)] = C["dg"]
        F1_VV = C["vv"]
        F1_DG = C["dg"]

    def _(v):
        W(v, "vv", F1_VV)
        v.tensor_copy(msl[:, 4:5], msl[:, 0:1])
        v.tensor_copy(msl[:, 5:6], msl[:, 2:3])
        I(v.drain(), "vv")
    VEC(_)
    crossmax(4, 1, "m_up")
    crossmax(5, 1, "m_gt")

    def _(v):
        W(v, "vv")
        v.tensor_copy(ccst[0:1, 0:1], sl("m_up"))
        v.tensor_copy(ccst[0:1, 1:2], sl("m_gt"))
        I(v.drain(), "vv")
    VEC(_)
    CC4 = collective()

    def _(v):
        W(v, "cc", CC4)
        v.tensor_copy(sl("gm_up"), ccrd[0:1, 0:1])
        v.tensor_copy(sl("gm_gt"), ccrd[0:1, 1:2])
        v.drain()
        v.tensor_scalar(sl("s_up"), sl("gm_up"), sl("rc32767"), 1e-12, OP.mult,
                        op1=OP.max)
        v.tensor_scalar(sl("s_g"), sl("gm_gt"), sl("rc32767"), 1e-12, OP.mult,
                        op1=OP.max)
        v.drain()
        v.reciprocal(sl("r_up"), sl("s_up"))
        v.reciprocal(sl("r_g"), sl("s_g"))
        I(v.drain(), "vv")
    VEC(_)
    bc_round(["r_up", "s_g", "r_g"])

    A.free("x2q", "wubf", "wgbf", "upst", "gtst")

    FWD = fence()
    wdbf = fv("wdbf", 128, NIC * D, BF16)
    WD_DONE = prep_ffn_weight("d", wd_n, wdT, wdbf, 96, NDC, NIC, C["vv"])
    A.free("indk")

    gst = fv("gst", 128, 2 * T, BF16)
    gqdq = fv("gqdq", 128, T)
    sgf = fv("sgf", 128, T)
    gss = fv("gss", 128, 2 * T, BF16)

    ldg = [C["vv"], C["vv"]]
    stg = [F1_DG, F1_DG]
    F2F = fence()
    for oc in range(NIC):
        par = oc % 2

        def _(sync, oc=oc, par=par, g=ldg[par]):
            W(sync, "vv", g)
            if oc < 2:
                wfence(sync, F2F)
                W(sync, "dg", F1_DG)
            dma(sync, gst[:, par * T:(par + 1) * T], gt_hp[oc * 128:(oc + 1) * 128, :],
                "dl")
        SYN(_)
        ld = C["dl"]

        def _(v, oc=oc, par=par, ld=ld):
            W(v, "dl", ld)
            for j in range(0, T, 512):
                v.tensor_scalar(qtmp[:, j:j + 512],
                                gst[:, par * T + j:par * T + j + 512], bca("r_g"),
                                MG, OP.mult, op1=OP.add)
                v.tensor_scalar(gqdq[:, j:j + 512], qtmp[:, j:j + 512], -MG,
                                bca("s_g"), OP.add, op1=OP.mult)
            I(v.drain(), "vv")
        VEC(_)
        s1 = C["vv"]
        ldg[par] = C["vv"]

        def _(s, s1=s1):
            W(s, "vv", s1)
            for j in range(0, T, 512):
                s.activation(sgf[:, j:j + 512], gqdq[:, j:j + 512], AF.Sigmoid)
            I(s.drain(), "ss")
        SCA(_)
        s2 = C["ss"]

        def _(v, oc=oc, par=par, s2=s2, g=stg[par]):
            W(v, "ss", s2)
            W(v, "dg", g)
            for j in range(0, T, 512):
                v.tensor_scalar(qtmp[:, j:j + 512], sgf[:, j:j + 512], 65536.0, MG,
                                OP.mult, op1=OP.add)
                v.tensor_scalar(sgf[:, j:j + 512], qtmp[:, j:j + 512], -MG,
                                1.0 / 65536.0, OP.add, op1=OP.mult)
                v.tensor_tensor(gss[:, par * T + j:par * T + j + 512],
                                sgf[:, j:j + 512], gqdq[:, j:j + 512], OP.mult)
            red512(v, msl[:, 7:8], gss[:, par * T:(par + 1) * T], T)
            if oc == 0:
                v.tensor_copy(msl[:, 6:7], msl[:, 7:8])
            else:
                v.tensor_tensor(msl[:, 6:7], msl[:, 6:7], msl[:, 7:8], OP.max)
            I(v.drain(), "vv")
        VEC(_)
        s3 = C["vv"]

        def _(g, oc=oc, par=par, s3=s3):
            W(g, "vv", s3)
            dma(g, gs_hp[oc * 128:(oc + 1) * 128, :], gss[:, par * T:(par + 1) * T],
                "dg")
        GPS(_)
        stg[par] = C["dg"]

    def _(v):
        W(v, "vv")
        v.tensor_copy(msl[:, 8:9], msl[:, 6:7])
        I(v.drain(), "vv")
    VEC(_)
    crossmax(8, 1, "m_gs")

    def _(v):
        W(v, "vv")
        v.tensor_copy(ccst[0:1, 0:1], sl("m_gs"))
        v.memset(ccst[0:1, 1:2], 0.0)
        I(v.drain(), "vv")
    VEC(_)
    CC5 = collective()

    def _(v):
        W(v, "cc", CC5)
        v.tensor_copy(sl("gm_gs"), ccrd[0:1, 0:1])
        v.drain()
        v.tensor_scalar(sl("s_gs"), sl("gm_gs"), sl("rc32767"), 1e-12, OP.mult,
                        op1=OP.max)
        v.drain()
        v.reciprocal(sl("r_gs"), sl("s_gs"))
        I(v.drain(), "vv")
    VEC(_)
    bc_round(["r_gs"])

    u3st = fv("u3st", 128, 2 * T, BF16)
    tst3 = fv("tst3", 128, 2 * T, BF16)
    ldg3 = [C["vv"], C["vv"]]
    stg3 = [C["dg"], C["dg"]]
    for oc in range(NIC):
        par = oc % 2

        def _(sync, oc=oc, par=par, g=ldg3[par]):
            W(sync, "vv", g)
            dma(sync, gst[:, par * T:(par + 1) * T], gs_hp[oc * 128:(oc + 1) * 128, :],
                "dl")
            dma(sync, u3st[:, par * T:(par + 1) * T], up_hp[oc * 128:(oc + 1) * 128, :],
                "dl")
        SYN(_)
        ld = C["dl"]

        def _(v, oc=oc, par=par, ld=ld, g=stg3[par]):
            W(v, "dl", ld)
            W(v, "dg", g)
            for j in range(0, T, 512):
                v.tensor_scalar(qtmp[:, j:j + 512],
                                gst[:, par * T + j:par * T + j + 512], bca("r_gs"),
                                MG, OP.mult, op1=OP.add)
                v.tensor_scalar(i16t[:, j:j + 512], qtmp[:, j:j + 512], -MG, None,
                                OP.add)
                v.tensor_scalar(qtmp[:, j:j + 512],
                                u3st[:, par * T + j:par * T + j + 512], bca("r_up"),
                                MG, OP.mult, op1=OP.add)
                v.tensor_scalar(qtmp[:, j:j + 512], qtmp[:, j:j + 512], -MG, None,
                                OP.add)
                v.tensor_tensor(tst3[:, par * T + j:par * T + j + 512],
                                i16t[:, j:j + 512], qtmp[:, j:j + 512], OP.mult)
            red512(v, msl[:, 10:11], tst3[:, par * T:(par + 1) * T], T)
            if oc == 0:
                v.tensor_copy(msl[:, 9:10], msl[:, 10:11])
            else:
                v.tensor_tensor(msl[:, 9:10], msl[:, 9:10], msl[:, 10:11], OP.max)
            I(v.drain(), "vv")
        VEC(_)
        s1 = C["vv"]
        ldg3[par] = C["vv"]

        def _(g, oc=oc, par=par, s1=s1):
            W(g, "vv", s1)
            dma(g, tp_hp[oc * 128:(oc + 1) * 128, :], tst3[:, par * T:(par + 1) * T],
                "dg")
        GPS(_)
        stg3[par] = C["dg"]

    def _(v):
        W(v, "vv")
        v.tensor_copy(msl[:, 11:12], msl[:, 9:10])
        I(v.drain(), "vv")
    VEC(_)
    crossmax(11, 1, "m_t")

    def _(v):
        W(v, "vv")
        v.tensor_copy(ccst[0:1, 0:1], sl("m_t"))
        I(v.drain(), "vv")
    VEC(_)
    CC6 = collective()

    def _(v):
        W(v, "cc", CC6)
        v.tensor_copy(sl("gm_t"), ccrd[0:1, 0:1])
        v.tensor_tensor(sl("sgu"), sl("s_gs"), sl("s_up"), OP.mult)
        v.drain()
        v.tensor_tensor(sl("oam"), sl("sgu"), sl("gm_t"), OP.mult)
        v.drain()
        v.tensor_scalar(sl("s_o"), sl("oam"), sl("rc32767"), 1e-12, OP.mult,
                        op1=OP.max)
        v.drain()
        v.reciprocal(sl("rs_o"), sl("s_o"))
        v.drain()
        v.tensor_tensor(sl("r_te"), sl("sgu"), sl("rs_o"), OP.mult)
        I(v.drain(), "vv")
    VEC(_)
    bc_round(["r_te", "s_o"])

    A.free("gst", "gqdq", "sgf", "gss", "u3st", "tst3")

    lnr = fv("lnr", 1, 6 * 512)
    obf = fv("obf", 128, NIC * 1024, BF16)
    x3 = fv("x3", 128, NDC * 1024)
    f4st = fv("f4st", 128, 2 * 1024, BF16)
    xr2 = fv("xr2", 128, 2 * 512)
    t1g = fv("t1g", 128, 512)
    sqg2 = fv("sqg2", 128, 2 * 512)
    F4F2 = fence()

    with nc.psum_tensor("psA4", [128, 512], F32) as psA, \
         nc.psum_tensor("psB4", [128, 512], F32) as psB, \
         nc.psum_tensor("psM4", [1, 512], F32) as psM, \
         nc.psum_tensor("psQ4", [1, 512], F32) as psQ, \
         nc.psum_tensor("psU4", [128, 512], F32) as psU, \
         nc.psum_tensor("psV4", [128, 512], F32) as psV:
        psAB = (psA, psB)

        def emit_ln4(xbuf, col_of, tts, ready_vv):
            mean = lnr[0:1, 0:512]
            msq = lnr[0:1, 512:1024]
            arow = lnr[0:1, 1024:1536]
            brow = lnr[0:1, 1536:2048]
            tmp = lnr[0:1, 2048:2560]
            for tt in tts:
                sqg = [C["pe"], C["pe"]]
                for oc in range(NDC):
                    par = oc % 2
                    sq = sqg2[:, par * 512:par * 512 + 512]

                    def _(t, oc=oc, tt=tt, ready_vv=ready_vv):
                        if oc == 0:
                            W(t, "vv", ready_vv)
                        I(t.matmul(psM[:, :], ones_c,
                                   xbuf[:, col_of(oc, tt):col_of(oc, tt) + 512],
                                   start=(oc == 0), stop=(oc == NDC - 1)), "pe")
                    TEN(_)

                    def _(v, sq=sq, oc=oc, tt=tt, g=sqg[par]):
                        W(v, "pe", g)
                        v.tensor_tensor(sq, xbuf[:, col_of(oc, tt):col_of(oc, tt) + 512],
                                        xbuf[:, col_of(oc, tt):col_of(oc, tt) + 512],
                                        OP.mult)
                        I(v.drain(), "vv")
                    VEC(_)
                    s1 = C["vv"]

                    def _(t, sq=sq, oc=oc, s1=s1):
                        W(t, "vv", s1)
                        I(t.matmul(psQ[:, :], ones_c, sq,
                                   start=(oc == 0), stop=(oc == NDC - 1)), "pe")
                    TEN(_)
                    sqg[par] = C["pe"]
                pes = C["pe"]

                def _(v, pes=pes):
                    W(v, "pe", pes)
                    v.tensor_scalar(mean, psM[:, :], 1.0 / 1024.0, None, OP.mult)
                    v.tensor_scalar(msq, psQ[:, :], 1.0 / 1024.0, None, OP.mult)
                    v.tensor_tensor(tmp, mean, mean, OP.mult)
                    v.tensor_tensor(msq, msq, tmp, OP.subtract)
                    PR(v)
                    I(v.drain(), "vv")
                VEC(_)
                s1 = C["vv"]

                def _(s, s1=s1):
                    W(s, "vv", s1)
                    s.activation(tmp, msq, AF.Ln, bias=eps_t[0:1, 0:1])
                    s.activation(arow, tmp, AF.Exp, scale=-0.5)
                    I(s.drain(), "ss")
                SCA(_)
                s1 = C["ss"]

                def _(v, s1=s1):
                    W(v, "ss", s1)
                    v.scalar_tensor_tensor(brow, mean, -1.0, arow, OP.mult, OP.mult)
                    I(v.drain(), "vv")
                VEC(_)
                s1 = C["vv"]

                def _(t, s1=s1):
                    W(t, "vv", s1)
                    t.matmul(psU[:, :], ones_r, arow, start=True, stop=True)
                    I(t.matmul(psV[:, :], ones_r, brow, start=True, stop=True), "pe")
                TEN(_)
                pes = C["pe"]

                def _(v, tt=tt, pes=pes):
                    W(v, "pe", pes)
                    for oc in range(NDC):
                        csl = xbuf[:, col_of(oc, tt):col_of(oc, tt) + 512]
                        v.tensor_tensor(t1g, csl, psU[:, :], OP.mult)
                        v.tensor_tensor(csl, t1g, psV[:, :], OP.add)
                    PR(v)
                    I(v.drain(), "vv")
                VEC(_)

        fst_g = [C["vv"], C["vv"]]
        for th in range(2):
            for ic in range(NIC):
                par = ic % 2

                def _(sync, th=th, ic=ic, par=par, g=fst_g[par]):
                    W(sync, "vv", g)
                    if th == 0 and ic < 2:
                        wfence(sync, F4F2)
                    dma(sync, f4st[:, par * 1024:(par + 1) * 1024],
                        tp_hp[ic * 128:(ic + 1) * 128, th * 1024:(th + 1) * 1024], "dl")
                SYN(_)
                ld = C["dl"]

                def _(v, ic=ic, par=par, ld=ld):
                    W(v, "dl", ld)
                    for j in range(0, 1024, 512):
                        v.tensor_scalar(qtmp[:, j:j + 512],
                                        f4st[:, par * 1024 + j:par * 1024 + j + 512],
                                        bca("r_te"), MG, OP.mult, op1=OP.add)
                        v.tensor_scalar(obf[:, ic * 1024 + j:ic * 1024 + j + 512],
                                        qtmp[:, j:j + 512], -MG, None, OP.add)
                    I(v.drain(), "vv")
                VEC(_)
                fst_g[par] = C["vv"]
            OQ_DONE = C["vv"]

            evg = [OQ_DONE, OQ_DONE]
            xrg = [OQ_DONE, OQ_DONE]
            for tt2 in range(2):
                for oc in range(NDC):
                    par = (tt2 * NDC + oc) % 2
                    ps = psAB[par]

                    def _(sync, th=th, tt2=tt2, oc=oc, par=par, g=xrg[par]):
                        W(sync, "vv", g)
                        dma(sync, xr2[:, par * 512:par * 512 + 512],
                            x2sp[oc * 128:(oc + 1) * 128,
                                 th * 1024 + tt2 * 512:th * 1024 + tt2 * 512 + 512],
                            "dl")
                    SYN(_)
                    ld = C["dl"]

                    def _(t, ps=ps, par=par, tt2=tt2, oc=oc, g=evg[par], od=OQ_DONE):
                        W(t, "vv", max(g, od))
                        for ic in range(NIC):
                            inst = t.matmul(
                                ps[:, :],
                                wdbf[:, ic * D + oc * 128:ic * D + oc * 128 + 128],
                                obf[:, ic * 1024 + tt2 * 512:ic * 1024 + tt2 * 512 + 512],
                                start=(ic == 0), stop=(ic == NIC - 1))
                        I(inst, "pe")
                    TEN(_)
                    pes = C["pe"]

                    def _(v, ps=ps, par=par, tt2=tt2, oc=oc, pes=pes, ld=ld):
                        W(v, "pe", pes)
                        W(v, "dl", ld)
                        v.tensor_scalar(t1g, ps[:, :], bca("s_o"), None, OP.mult)
                        v.tensor_tensor(
                            x3[:, oc * 1024 + tt2 * 512:oc * 1024 + tt2 * 512 + 512],
                            t1g, xr2[:, par * 512:par * 512 + 512], OP.add)
                        PR(v)
                        I(v.drain(), "vv")
                    VEC(_)
                    evg[par] = C["vv"]
                    xrg[par] = C["vv"]
            DOWN_DONE = C["vv"]

            emit_ln4(x3, lambda oc, tt2: oc * 1024 + tt2 * 512, range(2), DOWN_DONE)
            LN2_DONE = C["vv"]

            def _(g, th=th, s1=LN2_DONE):
                W(g, "vv", s1)
                for oc in range(NDC):
                    dma(g, outT[oc * 128:(oc + 1) * 128, th * 1024:(th + 1) * 1024],
                        x3[:, oc * 1024:(oc + 1) * 1024], "dg")
            GPS(_)
            fst_g = [C["dg"], C["dg"]]
            OUT_DG = C["dg"]

            def _(v, th=th, dgv=OUT_DG):
                W(v, "dg", dgv)
                I(v.drain(), "vv")
            VEC(_)
            fst_g = [C["vv"], C["vv"]]

    def _(g):
        g.wait_ge(SEMS["dg"], C["dg"])
    GPS(_)

    return nc


_NC_CACHE = None


def kernel(**inputs):
    global _NC_CACHE
    f32 = np.float32
    bf16 = ml_dtypes.bfloat16
    x = np.asarray(inputs["L0_attn_x"], dtype=f32)
    Wq = np.asarray(inputs["L0_attn_Wq"], dtype=f32)
    Wk = np.asarray(inputs["L0_attn_Wk"], dtype=f32)
    Wv = np.asarray(inputs["L0_attn_Wv"], dtype=f32)
    Wo = np.asarray(inputs["L0_attn_Wo"], dtype=f32)
    bq = np.asarray(inputs["L0_attn_bq"], dtype=f32)
    bk = np.asarray(inputs["L0_attn_bk"], dtype=f32)
    bo = np.asarray(inputs["L0_attn_bo"], dtype=f32)
    bv = np.asarray(inputs["L0_attn_bv"], dtype=f32)
    wg = np.asarray(inputs["L0_ffn_w_gate"], dtype=f32)
    wu = np.asarray(inputs["L0_ffn_w_up"], dtype=f32)
    wd = np.asarray(inputs["L0_ffn_w_down"], dtype=f32)

    cT = np.ascontiguousarray
    common = {
        "wqT": cT(Wq.T), "wkT": cT(Wk.T), "wvT": cT(Wv.T), "woT": cT(Wo.T),
        "wuT": cT(wu.T), "wgT": cT(wg.T), "wdT": cT(wd.T),
        "wu_n": cT(wu), "wg_n": cT(wg), "wd_n": cT(wd),
        "bqpp": cT(bq.reshape(NDC, 128).T), "bkpp": cT(bk.reshape(NDC, 128).T),
        "bopp": cT(bo.reshape(NDC, 128).T), "bvrow": cT(bv.reshape(1, D)),
        "identc": np.eye(128, dtype=f32),
    }
    ind16 = np.zeros((128, NDC * 16), dtype=f32)
    for dc in range(NDC):
        ind16[0:64, dc * 16 + 2 * dc] = 1.0
        ind16[64:128, dc * 16 + 2 * dc + 1] = 1.0
    common["ind16c"] = ind16.astype(bf16)
    indk = np.zeros((96, NIC * 128), dtype=f32)
    for ic in range(NIC):
        for g in range(4):
            indk[4 * ic + g, ic * 128 + 32 * g: ic * 128 + 32 * g + 32] = 1.0
    common["indkc"] = indk

    in_maps = []
    for b in range(NCORES):
        m = dict(common)
        m["xT"] = cT(x[b].T)
        in_maps.append(m)

    if _NC_CACHE is None:
        _NC_CACHE = build_nc()
    res = run_bass_kernel_spmd(_NC_CACHE, in_maps, core_ids=list(range(NCORES)))
    out = np.stack([np.asarray(res.results[b]["outT"], dtype=f32).T
                    for b in range(NCORES)], axis=0)
    return np.ascontiguousarray(out)


# version 10 (final)
# speedup vs baseline: 1.2325x; 1.2325x over previous
import numpy as np
import ml_dtypes
import concourse.bass as bass
import concourse.mybir as mybir
from concourse.bass_utils import run_bass_kernel_spmd
from concourse.alu_op_type import AluOpType as OP
from concourse.bass_isa_rust import ReduceOp

F32 = mybir.dt.float32
BF16 = mybir.dt.bfloat16
I8 = mybir.dt.int8
I16 = mybir.dt.int16
I32 = mybir.dt.int32
U16 = mybir.dt.uint16
AF = mybir.ActivationFunctionType
AX = mybir.AxisListType

D, T, DI, NH, NCORES = 1024, 2048, 3072, 16, 8
MG = 12582912.0
NDC = D // 128
NIC = DI // 128
NTT = T // 512
ISZ = {F32: 4, BF16: 2, I8: 1, I16: 2, I32: 4}
SB_BYTES = 208512


class Arena:
    def __init__(self, n):
        self.n, self.blocks, self.hw = n, [], 0

    def alloc(self, name, size):
        size = -(-size // 256) * 256
        off = 0
        for boff, bsize, _ in sorted(self.blocks):
            if off + size <= boff:
                break
            off = max(off, boff + bsize)
        assert off + size <= self.n, f"arena OOM: {name} {size}B at {off}"
        self.blocks.append((off, size, name))
        self.hw = max(self.hw, off + size)
        return off

    def free(self, *names):
        self.blocks = [b for b in self.blocks if b[2] not in names]


def build_nc():
    nc = bass.Bass("TRN2", target_bir_lowering=False, debug=False)
    nc.detect_race_conditions = False

    xT = nc.dram_tensor("xT", [D, T], F32, kind="ExternalInput").ap()
    wTd = {n: nc.dram_tensor("w" + n + "T", [D, D], F32, kind="ExternalInput").ap()
           for n in ("q", "k", "v", "o")}
    wuT = nc.dram_tensor("wuT", [D, DI], F32, kind="ExternalInput").ap()
    wgT = nc.dram_tensor("wgT", [D, DI], F32, kind="ExternalInput").ap()
    wdT = nc.dram_tensor("wdT", [DI, D], F32, kind="ExternalInput").ap()
    wu_n = nc.dram_tensor("wu_n", [DI, D], F32, kind="ExternalInput").ap()
    wg_n = nc.dram_tensor("wg_n", [DI, D], F32, kind="ExternalInput").ap()
    wd_n = nc.dram_tensor("wd_n", [D, DI], F32, kind="ExternalInput").ap()
    bqpp_d = nc.dram_tensor("bqpp", [128, NDC], F32, kind="ExternalInput").ap()
    bkpp_d = nc.dram_tensor("bkpp", [128, NDC], F32, kind="ExternalInput").ap()
    bopp_d = nc.dram_tensor("bopp", [128, NDC], F32, kind="ExternalInput").ap()
    bvrow_d = nc.dram_tensor("bvrow", [1, D], F32, kind="ExternalInput").ap()
    ident_d = nc.dram_tensor("identc", [128, 128], F32, kind="ExternalInput").ap()
    ind16_d = nc.dram_tensor("ind16c", [128, NDC * 16], BF16, kind="ExternalInput").ap()
    indk_d = nc.dram_tensor("indkc", [96, NIC * 128], F32, kind="ExternalInput").ap()
    outT = nc.dram_tensor("outT", [D, T], F32, kind="ExternalOutput").ap()

    cc_i = nc.dram_tensor("cc_i", [1, 128], F32, kind="Internal")
    cc_o = nc.dram_tensor("cc_o", [1, 128], F32, kind="Internal")
    x2sp = nc.dram_tensor("x2sp", [D, T], F32, kind="Internal").ap()
    y_hp = nc.dram_tensor("y_hp", [D, T], F32, kind="Internal").ap()
    up_hp = nc.dram_tensor("up_hp", [DI, T], BF16, kind="Internal").ap()
    gt_hp = nc.dram_tensor("gt_hp", [DI, T], BF16, kind="Internal").ap()
    gs_hp = nc.dram_tensor("gs_hp", [DI, T], BF16, kind="Internal").ap()
    tp_hp = nc.dram_tensor("tp_hp", [DI, T], BF16, kind="Internal").ap()

    from contextlib import ExitStack
    stack = ExitStack()
    i16t = stack.enter_context(nc.sbuf_tensor("i16t", [128, T], I16))[:]
    sbA = stack.enter_context(nc.sbuf_tensor("A", [128, SB_BYTES], I8))[:]
    A = Arena(SB_BYTES)
    CASTV = {dt: sbA.bitcast(dt) for dt in (F32, BF16, I8, I16, I32)}

    def fv(name, rows, cols, dt=F32):
        o = A.alloc(name, cols * ISZ[dt])
        e = o // ISZ[dt]
        return CASTV[dt][0:rows, e:e + cols]

    SEMS, C = {}, {}
    for s in ("dl", "dg", "pe", "vv", "ss", "gg", "cc", "pr"):
        SEMS[s] = stack.enter_context(nc.semaphore(s))
        C[s] = 0

    block = stack.enter_context(nc.Block())
    SECID = [0]

    def _secwrap(reg):
        def run(f):
            SECID[0] += 1
            reg(f)
        return run

    VEC, SCA, GPS, SYN = (_secwrap(block.vector), _secwrap(block.scalar),
                          _secwrap(block.gpsimd), _secwrap(block.sync))

    def TEN(f):
        pr_target = C["pr"]

        def g(t):
            t.wait_ge(SEMS["pr"], pr_target)
            f(t)
        block.tensor(g)

    def I(inst, name, n=1):
        inst.then_inc(SEMS[name], n)
        C[name] += n

    def W(eng, name, val=None):
        v = C[name] if val is None else val
        assert v <= C[name], f"wait on {name} for {v} > emitted {C[name]}"
        eng.wait_ge(SEMS[name], v)

    LASTDMA = {"dl": -1, "dg": -1}

    def dma(eng, out, in_, sem):
        if LASTDMA.get(sem) != SECID[0]:
            eng.wait_ge(SEMS[sem], C[sem])
            LASTDMA[sem] = SECID[0]
        eng.dma_start(out=out, in_=in_).then_inc(SEMS[sem], 16)
        C[sem] += 16

    def PRL(eng):
        I(eng.drain(), "pr")

    def PR(eng):
        pass

    def red512(v, dst, src, n):
        k = n // 512
        for j in range(k):
            v.tensor_reduce(msl[:, 24 + j:25 + j], src[:, j * 512:(j + 1) * 512],
                            AX.X, OP.max, apply_absolute_value=True)
        v.drain()
        v.tensor_reduce(dst, msl[:, 24:24 + k], AX.X, OP.max)

    def fence():
        return {k: C[k] for k in ("pe", "vv", "ss", "gg")}

    def wfence(eng, f):
        for k, v in f.items():
            eng.wait_ge(SEMS[k], v)

    ones_r = fv("ones_r", 1, 128)
    ones_c = fv("ones_c", 128, 1)
    ident = fv("ident", 128, 128)
    ind16 = fv("ind16", 128, NDC * 16, BF16)
    bqpp = fv("bqpp", 128, NDC)
    bkpp = fv("bkpp", 128, NDC)
    bopp = fv("bopp", 128, NDC)
    bvrow = fv("bvrow", 1, D)
    bvbc = fv("bvbc", 128, D)
    cst = fv("cst", 1, 16)
    sc = fv("sc", 1, 64)
    ccst = fv("ccst", 1, 128)
    ccrd = fv("ccrd", 1, 128)
    bcrow = fv("bcrow", 1, 32)
    bc = fv("bc", 128, 32)
    msl = fv("msl", 128, 32)
    msr = fv("msr", 1, 32)
    nmcol = fv("nmcol", 16, 8)
    nmrow = fv("nmrow", 1, 16)
    negM = fv("negM", 128, 16)
    eps_t = fv("eps_t", 1, 2)
    qtmp = fv("qtmp", 128, T)

    SLOT, BCJ = {}, {}

    def sl(name):
        j = SLOT.setdefault(name, len(SLOT))
        assert j < 64
        return sc[0:1, j:j + 1]

    def bcj(name):
        j = BCJ.setdefault(name, len(BCJ))
        assert j < 32
        return j

    def bca(name):
        j = bcj(name)
        return bc[:, j:j + 1]

    ps_bc = stack.enter_context(nc.psum_tensor("ps_bc", [128, 160], F32))

    def bc_round(names):
        js = sorted(bcj(n) for n in names)
        j0, j1 = js[0], js[-1] + 1

        def _(v):
            W(v, "vv")
            for n in names:
                v.tensor_copy(bcrow[0:1, bcj(n):bcj(n) + 1], sl(n))
            I(v.drain(), "vv")
        VEC(_)
        s1 = C["vv"]

        def _(t):
            W(t, "vv", s1)
            I(t.matmul(ps_bc[:, j0:j1], ones_r, bcrow[0:1, j0:j1],
                       start=True, stop=True), "pe")
        TEN(_)
        s2 = C["pe"]

        def _(v):
            W(v, "pe", s2)
            v.tensor_copy(bc[:, j0:j1], ps_bc[:, j0:j1])
            PR(v)
            I(v.drain(), "vv")
        VEC(_)
        return C["vv"]

    def collective():
        def _(g):
            W(g, "vv")
            dma(g, cc_i.ap(), ccst, "dg")
            g.wait_ge(SEMS["dg"], C["dg"])
            I(g.collective_compute("AllReduce", OP.max,
                                    replica_groups=[list(range(NCORES))],
                                    ins=[cc_i.ap()], outs=[cc_o.ap()]), "cc")
            g.wait_ge(SEMS["cc"], C["cc"])
            dma(g, ccrd, cc_o.ap(), "dg")
            g.wait_ge(SEMS["dg"], C["dg"])
            I(g.drain(), "cc")
        GPS(_)
        return C["cc"]

    def crossmax(src_j0, n, dst_slot):
        def _(v):
            W(v, "vv")
            v.tensor_reduce(msl[:, 30:31], msl[:, src_j0:src_j0 + n], AX.X, OP.max)
            I(v.drain(), "vv")
        VEC(_)
        s1 = C["vv"]

        def _(t):
            W(t, "vv", s1)
            I(t.transpose(ps_bc[0:1, 32:160], msl[:, 30:31], ident), "pe")
        TEN(_)
        s2 = C["pe"]

        def _(v):
            W(v, "pe", s2)
            v.tensor_reduce(sl(dst_slot), ps_bc[0:1, 32:160], AX.X, OP.max)
            PR(v)
            I(v.drain(), "vv")
        VEC(_)

    A.alloc("xs_pad", 2048)
    xs = fv("xs", 128, NDC * T)
    wstA = fv("wstA", 128, NDC * D)
    wstB = fv("wstB", 128, NDC * D)
    wqbf = fv("wqbf", 128, NDC * D, BF16)
    wkbf = fv("wkbf", 128, NDC * D, BF16)
    wvbf = fv("wvbf", 128, NDC * D, BF16)

    def _(sync):
        for dst, src in ((ident, ident_d), (ind16, ind16_d),
                         (bqpp, bqpp_d), (bkpp, bkpp_d), (bopp, bopp_d),
                         (bvrow, bvrow_d)):
            dma(sync, dst, src, "dl")
        for dc in range(NDC):
            dma(sync, xs[:, dc * T:(dc + 1) * T], xT[dc * 128:(dc + 1) * 128, :], "dl")
    SYN(_)
    LD_X = C["dl"]

    def _(v):
        v.memset(ones_r, 1.0)
        v.memset(ones_c, 1.0)
        v.memset(cst[0:1, 0:1], 127.0)
        v.memset(cst[0:1, 1:2], 32767.0)
        v.memset(cst[0:1, 2:3], 7.0)
        v.memset(eps_t[0:1, 0:1], 1e-5)
        v.memset(ccst, 0.0)
        v.drain()
        v.reciprocal(sl("rc127"), cst[0:1, 0:1])
        v.reciprocal(sl("rc32767"), cst[0:1, 1:2])
        v.reciprocal(sl("rc7"), cst[0:1, 2:3])
        W(v, "dl", LD_X)
        for dc in range(NDC):
            red512(v, msl[:, dc:dc + 1], xs[:, dc * T:(dc + 1) * T], T)
        I(v.drain(), "vv")
    VEC(_)
    crossmax(0, NDC, "mx_x")

    def _(v):
        W(v, "vv")
        v.tensor_copy(ccst[0:1, 0:1], sl("mx_x"))
        I(v.drain(), "vv")
    VEC(_)
    CC1 = collective()

    wslot = {"q": wstA, "k": wstB, "v": wstA}
    wbfs = {"q": wqbf, "k": wkbf, "v": wvbf}
    WQUANT_DONE = {}
    names3 = ("q", "k", "v")
    for wi, nm in enumerate(names3):
        st = wslot[nm]

        def _(sync, nm=nm, st=st, wi=wi):
            if wi >= 2:
                W(sync, "vv", WQUANT_DONE[names3[wi - 2]])
            for dc in range(NDC):
                dma(sync, st[:, dc * D:(dc + 1) * D],
                    wTd[nm][dc * 128:(dc + 1) * 128, :], "dl")
        SYN(_)
        ld = C["dl"]

        def _(v, nm=nm, st=st, ld=ld):
            W(v, "dl", ld)
            for dc in range(NDC):
                red512(v, msl[:, 8 + dc:9 + dc], st[:, dc * D:(dc + 1) * D], D)
            I(v.drain(), "vv")
        VEC(_)
        crossmax(8, NDC, "m_w" + nm)

        def _(v, nm=nm):
            W(v, "vv")
            v.tensor_scalar(sl("s_w" + nm), sl("m_w" + nm), sl("rc127"), 1e-8,
                            OP.mult, op1=OP.max)
            v.drain()
            v.reciprocal(sl("r_w" + nm), sl("s_w" + nm))
            I(v.drain(), "vv")
        VEC(_)
        bc_round(["r_w" + nm])

        def _(v, nm=nm, st=st):
            W(v, "vv")
            for dc in range(NDC):
                for j in range(0, D, 512):
                    v.tensor_scalar(qtmp[:, j:j + 512],
                                    st[:, dc * D + j:dc * D + j + 512],
                                    bca("r_w" + nm), MG, OP.mult, op1=OP.add)
                    v.tensor_scalar(wbfs[nm][:, dc * D + j:dc * D + j + 512],
                                    qtmp[:, j:j + 512], -MG, None, OP.add)
            I(v.drain(), "vv")
        VEC(_)
        WQUANT_DONE[nm] = C["vv"]

    def _(v):
        W(v, "cc", CC1)
        v.tensor_copy(sl("gm_x"), ccrd[0:1, 0:1])
        v.drain()
        v.tensor_scalar(sl("s_x"), sl("gm_x"), sl("rc127"), 1e-8, OP.mult, op1=OP.max)
        v.drain()
        v.reciprocal(sl("r_x"), sl("s_x"))
        v.tensor_tensor(sl("sxw_q"), sl("s_x"), sl("s_wq"), OP.mult)
        v.tensor_tensor(sl("sxw_k"), sl("s_x"), sl("s_wk"), OP.mult)
        v.tensor_tensor(sl("sxw_v"), sl("s_x"), sl("s_wv"), OP.mult)
        I(v.drain(), "vv")
    VEC(_)
    bc_round(["r_x", "sxw_q", "sxw_k", "sxw_v"])

    A.free("wstA", "wstB")
    xq = fv("xq", 128, NDC * T, BF16)

    def _(v):
        W(v, "vv")
        for dc in range(NDC):
            for j in range(0, T, 512):
                v.tensor_scalar(qtmp[:, j:j + 512],
                                xs[:, dc * T + j:dc * T + j + 512], bca("r_x"), MG,
                                OP.mult, op1=OP.add)
                v.tensor_scalar(xq[:, dc * T + j:dc * T + j + 512],
                                qtmp[:, j:j + 512], -MG, None, OP.add)
        I(v.drain(), "vv")
    VEC(_)
    A.free("xs", "xs_pad")

    F1F = fence()
    v65 = fv("v65", 128, 16 * NH * 65, BF16)
    v65r = v65.rearrange("p (t h c) -> p t h c", t=16, h=NH, c=65)
    qbf = fv("qbf", 128, NDC * T, BF16)
    kbf = fv("kbf", 128, NDC * T, BF16)
    sqt = fv("sqt", 128, 2 * 512, BF16)

    with nc.psum_tensor("psA1", [128, 512], F32) as psA, \
         nc.psum_tensor("psB1", [128, 512], F32) as psB, \
         nc.psum_tensor("psN1", [16, 2048], F32) as psN:
        psAB = (psA, psB)

        def _(t):
            W(t, "dl", LD_X)
            t.matmul(psA[:, :], ones_r, bvrow[0:1, 0:512], start=True, stop=True)
            I(t.matmul(psB[:, :], ones_r, bvrow[0:1, 512:1024], start=True, stop=True),
              "pe")
        TEN(_)
        s1 = C["pe"]

        def _(v):
            W(v, "pe", s1)
            v.tensor_copy(bvbc[:, 0:512], psA[:, :])
            v.tensor_copy(bvbc[:, 512:1024], psB[:, :])
            v.memset(v65r[:, :, :, 64:65], 1.0)
            PR(v)
            I(v.drain(), "vv")
        VEC(_)
        evg = [C["vv"], C["vv"]]

        for nm, obf, bpp in (("q", qbf, bqpp), ("k", kbf, bkpp)):
            for oc in range(NDC):
                for tt in range(NTT):
                    par = (oc * NTT + tt) % 2
                    ps = psAB[par]

                    def _(t, ps=ps, par=par, nm=nm, oc=oc, tt=tt):
                        W(t, "vv", evg[par])
                        for dc in range(NDC):
                            inst = t.matmul(
                                ps[:, :],
                                wbfs[nm][:, dc * D + oc * 128:dc * D + oc * 128 + 128],
                                xq[:, dc * T + tt * 512:dc * T + tt * 512 + 512],
                                start=(dc == 0), stop=(dc == NDC - 1))
                        I(inst, "pe")
                    TEN(_)
                    pes = C["pe"]

                    def _(v, ps=ps, obf=obf, nm=nm, bpp=bpp, oc=oc, tt=tt, pes=pes):
                        W(v, "pe", pes)
                        v.tensor_scalar(
                            obf[:, oc * T + tt * 512:oc * T + tt * 512 + 512],
                            ps[:, :], bca("sxw_" + nm), bpp[:, oc:oc + 1],
                            OP.mult, op1=OP.add)
                        PR(v)
                        I(v.drain(), "vv")
                    VEC(_)
                    evg[par] = C["vv"]

        for vt in range(16):
            for c2 in range(2):
                par = (vt * 2 + c2) % 2
                ps = psAB[par]

                def _(t, ps=ps, par=par, vt=vt, c2=c2):
                    W(t, "vv", evg[par])
                    for dc in range(NDC):
                        inst = t.matmul(
                            ps[:, :],
                            xq[:, dc * T + vt * 128:dc * T + vt * 128 + 128],
                            wvbf[:, dc * D + c2 * 512:dc * D + c2 * 512 + 512],
                            start=(dc == 0), stop=(dc == NDC - 1))
                    I(inst, "pe")
                TEN(_)
                pes = C["pe"]

                def _(v, ps=ps, vt=vt, c2=c2, pes=pes):
                    W(v, "pe", pes)
                    v.scalar_tensor_tensor(v65r[:, vt, 8 * c2:8 * c2 + 8, 0:64],
                                           ps[:, :], bca("sxw_v"),
                                           bvbc[:, c2 * 512:c2 * 512 + 512],
                                           OP.mult, OP.add)
                    PR(v)
                    I(v.drain(), "vv")
                VEC(_)
                evg[par] = C["vv"]

        QKV_DONE = C["vv"]

        psn_free = C["vv"]
        for si, obf in enumerate((qbf, kbf)):
            sqg = [C["pe"], C["pe"]]
            for tt in range(NTT):
                for dc in range(NDC):
                    par = dc % 2
                    sq = sqt[:, par * 512:par * 512 + 512]

                    def _(s, sq=sq, obf=obf, dc=dc, tt=tt, g=sqg[par]):
                        W(s, "pe", g)
                        if dc == 0 and tt == 0:
                            W(s, "vv", QKV_DONE)
                        s.activation(sq,
                                     obf[:, dc * T + tt * 512:dc * T + tt * 512 + 512],
                                     AF.Square)
                        I(s.drain(), "ss")
                    SCA(_)
                    sss = C["ss"]

                    def _(t, sq=sq, dc=dc, tt=tt, sss=sss, pf=psn_free):
                        W(t, "ss", sss)
                        if dc == 0 and tt == 0:
                            W(t, "vv", pf)
                        I(t.matmul(psN[:, tt * 512:tt * 512 + 512],
                                   ind16[:, dc * 16:dc * 16 + 16], sq,
                                   start=(dc == 0), stop=(dc == NDC - 1)), "pe")
                    TEN(_)
                    sqg[par] = C["pe"]
            pes = C["pe"]

            def _(v, si=si, pes=pes):
                W(v, "pe", pes)
                for j in range(4):
                    v.tensor_reduce(msl[0:16, 24 + j:25 + j],
                                    psN[:, j * 512:(j + 1) * 512], AX.X, OP.max)
                v.drain()
                v.tensor_reduce(nmcol[:, si:si + 1], msl[0:16, 24:28], AX.X, OP.max)
                PR(v)
                I(v.drain(), "vv")
            VEC(_)
            psn_free = C["vv"]

        def _(s):
            W(s, "vv")
            s.activation(nmcol[:, 2:3], nmcol[:, 0:1], AF.Sqrt)
            s.activation(nmcol[:, 3:4], nmcol[:, 1:2], AF.Sqrt)
            I(s.drain(), "ss")
        SCA(_)
        s1 = C["ss"]

        def _(v, s1=s1):
            W(v, "ss", s1)
            v.tensor_tensor(nmcol[:, 4:5], nmcol[:, 2:3], nmcol[:, 3:4], OP.mult)
            v.drain()
            v.tensor_scalar(nmcol[:, 5:6], nmcol[:, 4:5], -0.1275, -0.001,
                            OP.mult, op1=OP.add)
            I(v.drain(), "vv")
        VEC(_)
        s1 = C["vv"]

        def _(t, s1=s1):
            W(t, "vv", s1)
            I(t.transpose(ps_bc[0:1, 0:16], nmcol[0:16, 5:6], ident[0:16, 0:16]), "pe")
        TEN(_)
        s1 = C["pe"]

        def _(v, s1=s1):
            W(v, "pe", s1)
            v.tensor_copy(nmrow, ps_bc[0:1, 0:16])
            PR(v)
            I(v.drain(), "vv")
        VEC(_)
        s1 = C["vv"]

        def _(t, s1=s1):
            W(t, "vv", s1)
            I(t.matmul(ps_bc[:, 0:16], ones_r, nmrow, start=True, stop=True), "pe")
        TEN(_)
        s1 = C["pe"]

        def _(v, s1=s1):
            W(v, "pe", s1)
            v.tensor_copy(negM, ps_bc[:, 0:16])
            PR(v)
            I(v.drain(), "vv")
        VEC(_)
        NEGM_DONE = C["vv"]

    A.free("xq", "wqbf", "wkbf", "wvbf", "sqt", "bvrow", "bvbc")

    _pb1 = fv("Pb1", 128, 16 * 512, BF16)
    Pbuf = [fv("Pb0", 128, 12 * 512, BF16), _pb1]
    ystg = fv("ystg", 128, 2 * 512)
    rs_row = fv("rs_row", 1, 2 * 512)

    with nc.psum_tensor("psS0", [128, 512], F32) as psS0, \
         nc.psum_tensor("psS1", [128, 512], F32) as psS1, \
         nc.psum_tensor("psY0", [65, 512], F32) as psY0, \
         nc.psum_tensor("psY1", [65, 512], F32) as psY1, \
         nc.psum_tensor("psR0", [64, 512], F32) as psR0, \
         nc.psum_tensor("psR1", [64, 512], F32) as psR1:
        psS, psY, psR = (psS0, psS1), (psY0, psY1), (psR0, psR1)
        eg = [C["ss"], C["ss"]]
        pbg = [NEGM_DONE, NEGM_DONE]
        yg = [NEGM_DONE, NEGM_DONE]
        ystg_dg = [C["dg"], C["dg"]]

        def emit_scores(h, qt, pp):
            base, dch, ktn = 64 * (h % 2), h // 2, 4 * qt + 4
            for kt in range(ktn):
                par = kt % 2

                def _(t, h=h, qt=qt, kt=kt, par=par, base=base, dch=dch, pp=pp):
                    W(t, "ss", eg[par])
                    if kt == 0:
                        W(t, "vv", pbg[pp])
                    I(t.matmul(
                        psS[par][:, :],
                        kbf[base:base + 64, dch * T + kt * 128:dch * T + kt * 128 + 128],
                        qbf[base:base + 64, dch * T + qt * 512:dch * T + qt * 512 + 512],
                        start=True, stop=True), "pe")
                TEN(_)
                pes = C["pe"]

                def _(s, h=h, kt=kt, par=par, pp=pp, pes=pes):
                    W(s, "pe", pes)
                    if kt == 0:
                        W(s, "vv", NEGM_DONE)
                    s.activation(Pbuf[pp][:, kt * 512:kt * 512 + 512], psS[par][:, :],
                                 AF.Exp, bias=negM[:, h:h + 1], scale=0.125)
                    PR(s)
                    I(s.drain(), "ss")
                SCA(_)
                eg[par] = C["ss"]
            ssa = C["ss"]

            def _(g, qt=qt, pp=pp, ssa=ssa):
                W(g, "ss", ssa)
                for kt in range(4 * qt, 4 * qt + 4):
                    off = 128 * (kt - 4 * qt)
                    if off > 0:
                        g.memset(Pbuf[pp][:, kt * 512:kt * 512 + off], 0.0)
                    g.affine_select(Pbuf[pp][:, kt * 512 + off:kt * 512 + off + 128],
                                    Pbuf[pp][:, kt * 512 + off:kt * 512 + off + 128],
                                    [[1, 128]], OP.is_ge, 0.0,
                                    base=0, channel_multiplier=-1)
                I(g.drain(), "gg")
            GPS(_)
            return ssa, C["gg"]

        def emit_pv(h, qt, pp, rss, rgg):
            base, dch, ktn = 64 * (h % 2), h // 2, 4 * qt + 4
            par = (h * NTT + qt) % 2

            def _(t, h=h, pp=pp, par=par, ktn=ktn, rss=rss, rgg=rgg):
                W(t, "ss", rss)
                W(t, "gg", rgg)
                W(t, "vv", yg[par])
                for kt in range(ktn):
                    inst = t.matmul(psY[par][:, :], v65r[:, kt, h, :],
                                    Pbuf[pp][:, kt * 512:kt * 512 + 512],
                                    start=(kt == 0), stop=(kt == ktn - 1))
                I(inst, "pe")
            TEN(_)
            pes = C["pe"]
            rsl = rs_row[0:1, par * 512:par * 512 + 512]

            def _(v, par=par, pes=pes, rsl=rsl):
                W(v, "pe", pes)
                v.reciprocal(rsl, psY[par][64:65, :])
                PR(v)
                I(v.drain(), "vv")
            VEC(_)
            s1 = C["vv"]

            def _(t, par=par, s1=s1, rsl=rsl):
                W(t, "vv", s1)
                I(t.matmul(psR[par][:, :], ones_r[0:1, 0:64], rsl,
                           start=True, stop=True), "pe")
            TEN(_)
            pes2 = C["pe"]

            def _(v, par=par, pes2=pes2, g=ystg_dg[par]):
                W(v, "pe", pes2)
                W(v, "dg", g)
                ysl = ystg[0:64, par * 512:par * 512 + 512]
                v.tensor_copy(ysl, psY[par][0:64, :])
                v.tensor_tensor(ysl, ysl, psR[par][:, :], OP.mult)
                PR(v)
                I(v.drain(), "vv")
            VEC(_)
            yg[par] = C["vv"]
            s2 = C["vv"]

            def _(g, h=h, qt=qt, par=par, s2=s2):
                W(g, "vv", s2)
                dma(g, y_hp[64 * h:64 * h + 64, qt * 512:qt * 512 + 512],
                    ystg[0:64, par * 512:par * 512 + 512], "dg")
            GPS(_)
            ystg_dg[par] = C["dg"]
            return C["vv"]

        prev = None
        for h in range(NH):
            for qt in range(NTT):
                pp = qt % 2
                rss, rgg = emit_scores(h, qt, pp)
                if prev is not None:
                    pbg[prev[2]] = emit_pv(*prev)
                prev = (h, qt, pp, rss, rgg)
        emit_pv(*prev)
        Y_DONE = C["vv"]
        Y_DG = C["dg"]

    A.free("Pb0", "Pb1", "qbf", "kbf", "v65", "ystg")

    F3F = fence()
    yst = fv("yst", 128, 2 * T)

    ylg = [C["vv"], C["vv"]]
    for dc in range(NDC):
        par = dc % 2

        def _(sync, dc=dc, par=par, g=ylg[par]):
            W(sync, "vv", g)
            if dc < 2:
                wfence(sync, F3F)
                W(sync, "dg", Y_DG)
            dma(sync, yst[:, par * T:(par + 1) * T], y_hp[dc * 128:(dc + 1) * 128, :],
                "dl")
        SYN(_)
        ld = C["dl"]

        def _(v, dc=dc, par=par, ld=ld):
            W(v, "dl", ld)
            red512(v, msl[:, dc:dc + 1], yst[:, par * T:(par + 1) * T], T)
            I(v.drain(), "vv")
        VEC(_)
        ylg[par] = C["vv"]
    crossmax(0, NDC, "m_y")

    def _(v):
        W(v, "vv")
        v.tensor_copy(ccst[0:1, 0:1], sl("m_y"))
        I(v.drain(), "vv")
    VEC(_)
    CC2 = collective()

    wst3 = fv("wst3", 128, NDC * D)
    wobf = fv("wobf", 128, NDC * D, BF16)

    def _(sync):
        wfence(sync, F3F)
        for dc in range(NDC):
            dma(sync, wst3[:, dc * D:(dc + 1) * D], wTd["o"][dc * 128:(dc + 1) * 128, :],
                "dl")
    SYN(_)
    ld = C["dl"]

    def _(v, ld=ld):
        W(v, "dl", ld)
        for dc in range(NDC):
            red512(v, msl[:, 8 + dc:9 + dc], wst3[:, dc * D:(dc + 1) * D], D)
        I(v.drain(), "vv")
    VEC(_)
    crossmax(8, NDC, "m_wo")

    def _(v):
        W(v, "vv")
        v.tensor_scalar(sl("s_wo"), sl("m_wo"), sl("rc127"), 1e-8, OP.mult, op1=OP.max)
        v.drain()
        v.reciprocal(sl("r_wo"), sl("s_wo"))
        I(v.drain(), "vv")
    VEC(_)
    bc_round(["r_wo"])

    def _(v):
        W(v, "vv")
        for dc in range(NDC):
            for j in range(0, D, 512):
                v.tensor_scalar(qtmp[:, j:j + 512],
                                wst3[:, dc * D + j:dc * D + j + 512],
                                bca("r_wo"), MG, OP.mult, op1=OP.add)
                v.tensor_scalar(wobf[:, dc * D + j:dc * D + j + 512],
                                qtmp[:, j:j + 512], -MG, None, OP.add)
        I(v.drain(), "vv")
    VEC(_)
    A.free("wst3")

    def _(v):
        W(v, "cc", CC2)
        v.tensor_copy(sl("gm_y"), ccrd[0:1, 0:1])
        v.drain()
        v.tensor_scalar(sl("s_y"), sl("gm_y"), sl("rc127"), 1e-8, OP.mult, op1=OP.max)
        v.drain()
        v.reciprocal(sl("r_y"), sl("s_y"))
        v.tensor_tensor(sl("syw_o"), sl("s_y"), sl("s_wo"), OP.mult)
        I(v.drain(), "vv")
    VEC(_)
    bc_round(["r_y", "syw_o"])

    yq = fv("yq", 128, NDC * T, BF16)
    lnr = fv("lnr", 1, 6 * 512)

    yqg = [C["vv"], C["vv"]]
    for dc in range(NDC):
        par = dc % 2

        def _(sync, dc=dc, par=par, g=yqg[par]):
            W(sync, "vv", g)
            dma(sync, yst[:, par * T:(par + 1) * T], y_hp[dc * 128:(dc + 1) * 128, :],
                "dl")
        SYN(_)
        ld = C["dl"]

        def _(v, dc=dc, par=par, ld=ld):
            W(v, "dl", ld)
            W(v, "vv")
            for j in range(0, T, 512):
                v.tensor_scalar(qtmp[:, j:j + 512],
                                yst[:, par * T + j:par * T + j + 512], bca("r_y"),
                                MG, OP.mult, op1=OP.add)
                v.tensor_scalar(yq[:, dc * T + j:dc * T + j + 512],
                                qtmp[:, j:j + 512], -MG, None, OP.add)
            I(v.drain(), "vv")
        VEC(_)
        yqg[par] = C["vv"]
    YQ_DONE = C["vv"]
    A.free("yst")

    xs2 = fv("xs2", 128, NDC * T)
    xrr = fv("xrr", 128, 2 * 512)
    t1f = fv("t1f", 128, 512)
    sqf = fv("sqf", 128, 2 * 512)

    def emit_ln(xbuf, col_of, tts, psM, psQ, psU, psV, ready_vv):
        mean = lnr[0:1, 0:512]
        msq = lnr[0:1, 512:1024]
        arow = lnr[0:1, 1024:1536]
        brow = lnr[0:1, 1536:2048]
        tmp = lnr[0:1, 2048:2560]
        for tt in tts:
            sqg = [C["pe"], C["pe"]]
            for oc in range(NDC):
                par = oc % 2
                sq = sqf[:, par * 512:par * 512 + 512]

                def _(t, oc=oc, tt=tt, ready_vv=ready_vv):
                    if oc == 0:
                        W(t, "vv", ready_vv)
                    I(t.matmul(psM[:, :], ones_c,
                               xbuf[:, col_of(oc, tt):col_of(oc, tt) + 512],
                               start=(oc == 0), stop=(oc == NDC - 1)), "pe")
                TEN(_)

                def _(v, sq=sq, oc=oc, tt=tt, g=sqg[par], ready_vv=ready_vv):
                    W(v, "pe", g)
                    v.tensor_tensor(sq, xbuf[:, col_of(oc, tt):col_of(oc, tt) + 512],
                                    xbuf[:, col_of(oc, tt):col_of(oc, tt) + 512],
                                    OP.mult)
                    I(v.drain(), "vv")
                VEC(_)
                s1 = C["vv"]

                def _(t, sq=sq, oc=oc, s1=s1):
                    W(t, "vv", s1)
                    I(t.matmul(psQ[:, :], ones_c, sq,
                               start=(oc == 0), stop=(oc == NDC - 1)), "pe")
                TEN(_)
                sqg[par] = C["pe"]
            pes = C["pe"]

            def _(v, pes=pes):
                W(v, "pe", pes)
                v.tensor_scalar(mean, psM[:, :], 1.0 / 1024.0, None, OP.mult)
                v.tensor_scalar(msq, psQ[:, :], 1.0 / 1024.0, None, OP.mult)
                v.tensor_tensor(tmp, mean, mean, OP.mult)
                v.tensor_tensor(msq, msq, tmp, OP.subtract)
                PRL(v)
                I(v.drain(), "vv")
            VEC(_)
            s1 = C["vv"]

            def _(s, s1=s1):
                W(s, "vv", s1)
                s.activation(tmp, msq, AF.Ln, bias=eps_t[0:1, 0:1])
                s.activation(arow, tmp, AF.Exp, scale=-0.5)
                I(s.drain(), "ss")
            SCA(_)
            s1 = C["ss"]

            def _(v, s1=s1):
                W(v, "ss", s1)
                v.scalar_tensor_tensor(brow, mean, -1.0, arow, OP.mult, OP.mult)
                I(v.drain(), "vv")
            VEC(_)
            s1 = C["vv"]

            def _(t, s1=s1):
                W(t, "vv", s1)
                t.matmul(psU[:, :], ones_r, arow, start=True, stop=True)
                I(t.matmul(psV[:, :], ones_r, brow, start=True, stop=True), "pe")
            TEN(_)
            pes = C["pe"]

            def _(v, tt=tt, pes=pes):
                W(v, "pe", pes)
                for oc in range(NDC):
                    csl = xbuf[:, col_of(oc, tt):col_of(oc, tt) + 512]
                    v.tensor_tensor(t1f, csl, psU[:, :], OP.mult)
                    v.tensor_tensor(csl, t1f, psV[:, :], OP.add)
                PRL(v)
                I(v.drain(), "vv")
            VEC(_)

    with nc.psum_tensor("psA3", [128, 512], F32) as psA, \
         nc.psum_tensor("psB3", [128, 512], F32) as psB, \
         nc.psum_tensor("psM3", [1, 512], F32) as psM, \
         nc.psum_tensor("psQ3", [1, 512], F32) as psQ, \
         nc.psum_tensor("psU3", [128, 512], F32) as psU, \
         nc.psum_tensor("psV3", [128, 512], F32) as psV:
        psAB = (psA, psB)
        evg = [YQ_DONE, YQ_DONE]
        xrg = [YQ_DONE, YQ_DONE]
        for oc in range(NDC):
            for tt in range(NTT):
                par = (oc * NTT + tt) % 2
                ps = psAB[par]

                def _(sync, oc=oc, tt=tt, par=par):
                    W(sync, "vv", xrg[par])
                    dma(sync, xrr[:, par * 512:par * 512 + 512],
                        xT[oc * 128:oc * 128 + 128, tt * 512:tt * 512 + 512], "dl")
                SYN(_)
                ld = C["dl"]

                def _(t, ps=ps, par=par, oc=oc, tt=tt):
                    W(t, "vv", evg[par])
                    for dc in range(NDC):
                        inst = t.matmul(
                            ps[:, :],
                            wobf[:, dc * D + oc * 128:dc * D + oc * 128 + 128],
                            yq[:, dc * T + tt * 512:dc * T + tt * 512 + 512],
                            start=(dc == 0), stop=(dc == NDC - 1))
                    I(inst, "pe")
                TEN(_)
                pes = C["pe"]

                def _(v, ps=ps, par=par, oc=oc, tt=tt, pes=pes, ld=ld):
                    W(v, "pe", pes)
                    W(v, "dl", ld)
                    v.tensor_scalar(t1f, ps[:, :], bca("syw_o"), bopp[:, oc:oc + 1],
                                    OP.mult, op1=OP.add)
                    v.tensor_tensor(xs2[:, oc * T + tt * 512:oc * T + tt * 512 + 512],
                                    t1f, xrr[:, par * 512:par * 512 + 512], OP.add)
                    PRL(v)
                    I(v.drain(), "vv")
                VEC(_)
                evg[par] = C["vv"]
                xrg[par] = C["vv"]
        OPROJ_DONE = C["vv"]

        emit_ln(xs2, lambda oc, tt: oc * T + tt * 512, range(NTT),
                psM, psQ, psU, psV, OPROJ_DONE)
        LN1_DONE = C["vv"]

    def _(v):
        W(v, "vv", LN1_DONE)
        for dc in range(NDC):
            red512(v, msl[:, dc:dc + 1], xs2[:, dc * T:(dc + 1) * T], T)
        I(v.drain(), "vv")
    VEC(_)
    crossmax(0, NDC, "m_x2")

    def _(v):
        W(v, "vv")
        v.tensor_copy(ccst[0:1, 0:1], sl("m_x2"))
        I(v.drain(), "vv")
    VEC(_)
    CC3 = collective()

    def _(v):
        W(v, "cc", CC3)
        v.tensor_copy(sl("gm_x2"), ccrd[0:1, 0:1])
        v.drain()
        v.tensor_scalar(sl("s_x2"), sl("gm_x2"), sl("rc32767"), 1e-12, OP.mult,
                        op1=OP.max)
        v.drain()
        v.reciprocal(sl("r_x2"), sl("s_x2"))
        v.tensor_copy(sl("rc7b"), sl("rc7"))
        I(v.drain(), "vv")
    VEC(_)
    bc_round(["r_x2", "s_x2", "rc7b"])

    A.free("yq", "wobf", "xrr", "t1f", "sqf", "lnr")
    x2q = fv("x2q", 128, NDC * T, BF16)

    def _(v):
        W(v, "vv")
        for dc in range(NDC):
            for j in range(0, T, 512):
                v.tensor_scalar(qtmp[:, j:j + 512],
                                xs2[:, dc * T + j:dc * T + j + 512], bca("r_x2"),
                                MG, OP.mult, op1=OP.add)
                v.tensor_scalar(x2q[:, dc * T + j:dc * T + j + 512],
                                qtmp[:, j:j + 512], -MG, None, OP.add)
        I(v.drain(), "vv")
    VEC(_)
    X2Q_DONE = C["vv"]

    def _(g):
        W(g, "vv", X2Q_DONE)
        for dc in range(NDC):
            dma(g, x2sp[dc * 128:(dc + 1) * 128, :], xs2[:, dc * T:(dc + 1) * T], "dg")
    GPS(_)
    X2SPILL = C["dg"]
    A.free("xs2")

    F4F = fence()
    indk = fv("indk", 96, NIC * 128)

    def _(sync):
        wfence(sync, F4F)
        dma(sync, indk, indk_d, "dl")
    SYN(_)

    def prep_ffn_weight(nm, w_native, w_trans, wbf, nblk, n_ot, n_ic, prev_done):
        O = n_ot * 128
        nst = fv("nst" + nm, 128, nblk * 32)
        tst = fv("tst" + nm, 128, O)
        sT = fv("sT" + nm, nblk, O)
        rT = fv("rT" + nm, nblk, O)
        bm = fv("bm" + nm, 128, nblk)
        ssc = fv("ssc" + nm, 128, nblk)
        rsc = fv("rsc" + nm, 128, nblk)
        I1 = nblk * 32
        with nc.psum_tensor("psTs" + nm, [nblk, 128], F32) as psTs, \
             nc.psum_tensor("psTr" + nm, [nblk, 128], F32) as psTr, \
             nc.psum_tensor("psRB" + nm, [128, 512], F32) as psRB, \
             nc.psum_tensor("psSB" + nm, [128, 512], F32) as psSB:
            nstg = [prev_done]
            tcopy_vv = C["vv"]
            trps = [C["pe"]]
            for ot in range(n_ot):
                def _(sync, ot=ot):
                    W(sync, "vv", nstg[0])
                    if ot < 2:
                        wfence(sync, F4F)
                    dma(sync, nst, w_native[ot * 128:(ot + 1) * 128, :], "dl")
                SYN(_)
                ld = C["dl"]

                def _(v, ld=ld, nblk=nblk):
                    W(v, "dl", ld)
                    W(v, "pe", trps[0])
                    r3 = nst.rearrange("p (b i) -> p b i", i=32)
                    for j in range(0, nblk, 16):
                        v.tensor_reduce(bm[:, j:j + 16], r3[:, j:j + 16, :],
                                        AX.X, OP.max, apply_absolute_value=True)
                    v.drain()
                    v.tensor_scalar(ssc[:, :], bm[:, :], bca("rc7b"), 1e-12,
                                    OP.mult, op1=OP.max)
                    v.drain()
                    v.reciprocal(rsc[:, :], ssc[:, :])
                    I(v.drain(), "vv")
                VEC(_)
                nstg[0] = C["vv"]
                s1 = C["vv"]

                def _(t, s1=s1, g=tcopy_vv):
                    W(t, "vv", max(s1, g))
                    t.transpose(psTs[:, :], ssc[:, :], ident)
                    I(t.transpose(psTr[:, :], rsc[:, :], ident), "pe")
                TEN(_)
                pes = C["pe"]
                trps[0] = pes

                def _(v, ot=ot, pes=pes):
                    W(v, "pe", pes)
                    v.tensor_copy(sT[:, ot * 128:(ot + 1) * 128], psTs[:, :])
                    v.tensor_copy(rT[:, ot * 128:(ot + 1) * 128], psTr[:, :])
                    PR(v)
                    I(v.drain(), "vv")
                VEC(_)
                tcopy_vv = C["vv"]
            SCALES_DONE = C["vv"]

            tstg = [SCALES_DONE]
            bcg = [C["vv"]]
            for ic in range(n_ic):
                def _(sync, ic=ic):
                    W(sync, "vv", tstg[0])
                    dma(sync, tst, w_trans[ic * 128:(ic + 1) * 128, :], "dl")
                SYN(_)
                ld = C["dl"]
                for os_ in range(O // 512):
                    def _(t, ic=ic, os_=os_, sd=SCALES_DONE, g=bcg[0]):
                        W(t, "vv", max(sd, g))
                        t.matmul(psRB[:, :], indk[0:nblk, ic * 128:(ic + 1) * 128],
                                 rT[:, os_ * 512:os_ * 512 + 512],
                                 start=True, stop=True)
                        I(t.matmul(psSB[:, :], indk[0:nblk, ic * 128:(ic + 1) * 128],
                                   sT[:, os_ * 512:os_ * 512 + 512],
                                   start=True, stop=True), "pe")
                    TEN(_)
                    pes = C["pe"]

                    def _(v, ic=ic, os_=os_, pes=pes, ld=ld, wbf=wbf, O=O):
                        W(v, "pe", pes)
                        W(v, "dl", ld)
                        v.tensor_tensor(qtmp[:, 0:512],
                                        tst[:, os_ * 512:os_ * 512 + 512],
                                        psRB[:, :], OP.mult)
                        v.tensor_scalar(qtmp[:, 0:512], qtmp[:, 0:512], MG, -MG,
                                        OP.add, op1=OP.add)
                        v.tensor_tensor(wbf[:, ic * O + os_ * 512:ic * O + os_ * 512 + 512],
                                        qtmp[:, 0:512], psSB[:, :], OP.mult)
                        PR(v)
                        I(v.drain(), "vv")
                    VEC(_)
                    bcg[0] = C["vv"]
                tstg[0] = C["vv"]
        A.free("nst" + nm, "tst" + nm, "sT" + nm, "rT" + nm,
               "bm" + nm, "ssc" + nm, "rsc" + nm)
        return C["vv"]

    wubf = fv("wubf", 128, NDC * DI, BF16)
    wgbf = fv("wgbf", 128, NDC * DI, BF16)
    dn1 = prep_ffn_weight("u", wu_n, wuT, wubf, 32, NIC, NDC, C["vv"])
    dn2 = prep_ffn_weight("g", wg_n, wgT, wgbf, 32, NIC, NDC, dn1)
    WU_DONE = C["vv"]

    upst = fv("upst", 128, 2 * T, BF16)
    gtst = fv("gtst", 128, 2 * T, BF16)

    with nc.psum_tensor("psUA", [128, 512], F32) as psUA, \
         nc.psum_tensor("psUB", [128, 512], F32) as psUB, \
         nc.psum_tensor("psGA", [128, 512], F32) as psGA, \
         nc.psum_tensor("psGB", [128, 512], F32) as psGB:
        psU2 = (psUA, psUB)
        psG2 = (psGA, psGB)
        evgU = [C["ss"], C["ss"]]
        evgG = [C["ss"], C["ss"]]
        stage_vv = {("u", 0): 0, ("u", 1): 0, ("g", 0): 0, ("g", 1): 0}
        stage_dg = dict(stage_vv)
        first_mm = True
        first_evac = True
        for oc in range(NIC):
            spar = oc % 2
            for tensor_nm, psP, evg, wbf, stage, hbuf in (
                    ("u", psU2, evgU, wubf, upst, up_hp),
                    ("g", psG2, evgG, wgbf, gtst, gt_hp)):
                for tt in range(NTT):
                    par = tt % 2
                    ps = psP[par]

                    def _(t, ps=ps, wbf=wbf, oc=oc, tt=tt, g=evg[par], fm=first_mm):
                        W(t, "ss", g)
                        if fm:
                            W(t, "vv", WU_DONE)
                        for ic in range(NDC):
                            inst = t.matmul(
                                ps[:, :],
                                wbf[:, ic * DI + oc * 128:ic * DI + oc * 128 + 128],
                                x2q[:, ic * T + tt * 512:ic * T + tt * 512 + 512],
                                start=(ic == 0), stop=(ic == NDC - 1))
                        I(inst, "pe")
                    TEN(_)
                    first_mm = False
                    pes = C["pe"]

                    def _(s, ps=ps, stage=stage, spar=spar, tt=tt, pes=pes,
                          fe=first_evac, gvv=stage_vv[(tensor_nm, spar)],
                          gdg=stage_dg[(tensor_nm, spar)]):
                        W(s, "pe", pes)
                        if fe:
                            W(s, "dg", X2SPILL)
                        if tt == 0:
                            s.wait_ge(SEMS["vv"], gvv)
                            s.wait_ge(SEMS["dg"], gdg)
                        s.mul(stage[:, spar * T + tt * 512:spar * T + tt * 512 + 512],
                              ps[:, :], bca("s_x2"))
                        PR(s)
                        I(s.drain(), "ss")
                    SCA(_)
                    first_evac = False
                    evg[par] = C["ss"]
                ssa = C["ss"]
                slot = 0 if tensor_nm == "u" else 2

                def _(v, stage=stage, spar=spar, oc=oc, slot=slot, ssa=ssa):
                    W(v, "ss", ssa)
                    red512(v, msl[:, slot + 1:slot + 2],
                           stage[:, spar * T:(spar + 1) * T], T)
                    if oc == 0:
                        v.tensor_copy(msl[:, slot:slot + 1], msl[:, slot + 1:slot + 2])
                    else:
                        v.tensor_tensor(msl[:, slot:slot + 1], msl[:, slot:slot + 1],
                                        msl[:, slot + 1:slot + 2], OP.max)
                    I(v.drain(), "vv")
                VEC(_)

                def _(g, stage=stage, spar=spar, oc=oc, hbuf=hbuf, ssa=ssa):
                    W(g, "ss", ssa)
                    dma(g, hbuf[oc * 128:(oc + 1) * 128, :],
                        stage[:, spar * T:(spar + 1) * T], "dg")
                GPS(_)
                stage_vv[(tensor_nm, spar)] = C["vv"]
                stage_dg[(tensor_nm, spar)] = C["dg"]
        F1_VV = C["vv"]
        F1_DG = C["dg"]

    def _(v):
        W(v, "vv", F1_VV)
        v.tensor_copy(msl[:, 4:5], msl[:, 0:1])
        v.tensor_copy(msl[:, 5:6], msl[:, 2:3])
        I(v.drain(), "vv")
    VEC(_)
    crossmax(4, 1, "m_up")
    crossmax(5, 1, "m_gt")

    def _(v):
        W(v, "vv")
        v.tensor_copy(ccst[0:1, 0:1], sl("m_up"))
        v.tensor_copy(ccst[0:1, 1:2], sl("m_gt"))
        I(v.drain(), "vv")
    VEC(_)
    CC4 = collective()

    def _(v):
        W(v, "cc", CC4)
        v.tensor_copy(sl("gm_up"), ccrd[0:1, 0:1])
        v.tensor_copy(sl("gm_gt"), ccrd[0:1, 1:2])
        v.drain()
        v.tensor_scalar(sl("s_up"), sl("gm_up"), sl("rc32767"), 1e-12, OP.mult,
                        op1=OP.max)
        v.tensor_scalar(sl("s_g"), sl("gm_gt"), sl("rc32767"), 1e-12, OP.mult,
                        op1=OP.max)
        v.drain()
        v.reciprocal(sl("r_up"), sl("s_up"))
        v.reciprocal(sl("r_g"), sl("s_g"))
        I(v.drain(), "vv")
    VEC(_)
    bc_round(["r_up", "s_g", "r_g"])

    A.free("x2q", "wubf", "wgbf", "upst", "gtst")

    FWD = fence()
    wdbf = fv("wdbf", 128, NIC * D, BF16)
    WD_DONE = prep_ffn_weight("d", wd_n, wdT, wdbf, 96, NDC, NIC, C["vv"])
    A.free("indk")

    gst = fv("gst", 128, 2 * T, BF16)
    gqdq = fv("gqdq", 128, T)
    sgf = fv("sgf", 128, T)
    gss = fv("gss", 128, 2 * T, BF16)

    ldg = [C["vv"], C["vv"]]
    stg = [F1_DG, F1_DG]
    F2F = fence()
    for oc in range(NIC):
        par = oc % 2

        def _(sync, oc=oc, par=par, g=ldg[par]):
            W(sync, "vv", g)
            if oc < 2:
                wfence(sync, F2F)
                W(sync, "dg", F1_DG)
            dma(sync, gst[:, par * T:(par + 1) * T], gt_hp[oc * 128:(oc + 1) * 128, :],
                "dl")
        SYN(_)
        ld = C["dl"]

        def _(v, oc=oc, par=par, ld=ld):
            W(v, "dl", ld)
            for j in range(0, T, 512):
                v.tensor_scalar(qtmp[:, j:j + 512],
                                gst[:, par * T + j:par * T + j + 512], bca("r_g"),
                                MG, OP.mult, op1=OP.add)
                v.tensor_scalar(gqdq[:, j:j + 512], qtmp[:, j:j + 512], -MG,
                                bca("s_g"), OP.add, op1=OP.mult)
            I(v.drain(), "vv")
        VEC(_)
        s1 = C["vv"]
        ldg[par] = C["vv"]

        def _(s, s1=s1):
            W(s, "vv", s1)
            for j in range(0, T, 512):
                s.activation(sgf[:, j:j + 512], gqdq[:, j:j + 512], AF.Sigmoid)
            I(s.drain(), "ss")
        SCA(_)
        s2 = C["ss"]

        def _(v, oc=oc, par=par, s2=s2, g=stg[par]):
            W(v, "ss", s2)
            W(v, "dg", g)
            for j in range(0, T, 512):
                v.tensor_scalar(qtmp[:, j:j + 512], sgf[:, j:j + 512], 65536.0, MG,
                                OP.mult, op1=OP.add)
                v.tensor_scalar(sgf[:, j:j + 512], qtmp[:, j:j + 512], -MG,
                                1.0 / 65536.0, OP.add, op1=OP.mult)
                v.tensor_tensor(gss[:, par * T + j:par * T + j + 512],
                                sgf[:, j:j + 512], gqdq[:, j:j + 512], OP.mult)
            red512(v, msl[:, 7:8], gss[:, par * T:(par + 1) * T], T)
            if oc == 0:
                v.tensor_copy(msl[:, 6:7], msl[:, 7:8])
            else:
                v.tensor_tensor(msl[:, 6:7], msl[:, 6:7], msl[:, 7:8], OP.max)
            I(v.drain(), "vv")
        VEC(_)
        s3 = C["vv"]

        def _(g, oc=oc, par=par, s3=s3):
            W(g, "vv", s3)
            dma(g, gs_hp[oc * 128:(oc + 1) * 128, :], gss[:, par * T:(par + 1) * T],
                "dg")
        GPS(_)
        stg[par] = C["dg"]

    def _(v):
        W(v, "vv")
        v.tensor_copy(msl[:, 8:9], msl[:, 6:7])
        I(v.drain(), "vv")
    VEC(_)
    crossmax(8, 1, "m_gs")

    def _(v):
        W(v, "vv")
        v.tensor_copy(ccst[0:1, 0:1], sl("m_gs"))
        v.memset(ccst[0:1, 1:2], 0.0)
        I(v.drain(), "vv")
    VEC(_)
    CC5 = collective()

    def _(v):
        W(v, "cc", CC5)
        v.tensor_copy(sl("gm_gs"), ccrd[0:1, 0:1])
        v.drain()
        v.tensor_scalar(sl("s_gs"), sl("gm_gs"), sl("rc32767"), 1e-12, OP.mult,
                        op1=OP.max)
        v.drain()
        v.reciprocal(sl("r_gs"), sl("s_gs"))
        I(v.drain(), "vv")
    VEC(_)
    bc_round(["r_gs"])

    u3st = fv("u3st", 128, 2 * T, BF16)
    tst3 = fv("tst3", 128, 2 * T, BF16)
    ldg3 = [C["vv"], C["vv"]]
    stg3 = [C["dg"], C["dg"]]
    for oc in range(NIC):
        par = oc % 2

        def _(sync, oc=oc, par=par, g=ldg3[par]):
            W(sync, "vv", g)
            dma(sync, gst[:, par * T:(par + 1) * T], gs_hp[oc * 128:(oc + 1) * 128, :],
                "dl")
            dma(sync, u3st[:, par * T:(par + 1) * T], up_hp[oc * 128:(oc + 1) * 128, :],
                "dl")
        SYN(_)
        ld = C["dl"]

        def _(v, oc=oc, par=par, ld=ld, g=stg3[par]):
            W(v, "dl", ld)
            W(v, "dg", g)
            for j in range(0, T, 512):
                v.tensor_scalar(qtmp[:, j:j + 512],
                                gst[:, par * T + j:par * T + j + 512], bca("r_gs"),
                                MG, OP.mult, op1=OP.add)
                v.tensor_scalar(i16t[:, j:j + 512], qtmp[:, j:j + 512], -MG, None,
                                OP.add)
                v.tensor_scalar(qtmp[:, j:j + 512],
                                u3st[:, par * T + j:par * T + j + 512], bca("r_up"),
                                MG, OP.mult, op1=OP.add)
                v.tensor_scalar(qtmp[:, j:j + 512], qtmp[:, j:j + 512], -MG, None,
                                OP.add)
                v.tensor_tensor(tst3[:, par * T + j:par * T + j + 512],
                                i16t[:, j:j + 512], qtmp[:, j:j + 512], OP.mult)
            red512(v, msl[:, 10:11], tst3[:, par * T:(par + 1) * T], T)
            if oc == 0:
                v.tensor_copy(msl[:, 9:10], msl[:, 10:11])
            else:
                v.tensor_tensor(msl[:, 9:10], msl[:, 9:10], msl[:, 10:11], OP.max)
            I(v.drain(), "vv")
        VEC(_)
        s1 = C["vv"]
        ldg3[par] = C["vv"]

        def _(g, oc=oc, par=par, s1=s1):
            W(g, "vv", s1)
            dma(g, tp_hp[oc * 128:(oc + 1) * 128, :], tst3[:, par * T:(par + 1) * T],
                "dg")
        GPS(_)
        stg3[par] = C["dg"]

    def _(v):
        W(v, "vv")
        v.tensor_copy(msl[:, 11:12], msl[:, 9:10])
        I(v.drain(), "vv")
    VEC(_)
    crossmax(11, 1, "m_t")

    def _(v):
        W(v, "vv")
        v.tensor_copy(ccst[0:1, 0:1], sl("m_t"))
        I(v.drain(), "vv")
    VEC(_)
    CC6 = collective()

    def _(v):
        W(v, "cc", CC6)
        v.tensor_copy(sl("gm_t"), ccrd[0:1, 0:1])
        v.tensor_tensor(sl("sgu"), sl("s_gs"), sl("s_up"), OP.mult)
        v.drain()
        v.tensor_tensor(sl("oam"), sl("sgu"), sl("gm_t"), OP.mult)
        v.drain()
        v.tensor_scalar(sl("s_o"), sl("oam"), sl("rc32767"), 1e-12, OP.mult,
                        op1=OP.max)
        v.drain()
        v.reciprocal(sl("rs_o"), sl("s_o"))
        v.drain()
        v.tensor_tensor(sl("r_te"), sl("sgu"), sl("rs_o"), OP.mult)
        I(v.drain(), "vv")
    VEC(_)
    bc_round(["r_te", "s_o"])

    A.free("gst", "gqdq", "sgf", "gss", "u3st", "tst3")

    lnr = fv("lnr", 1, 6 * 512)
    obf = fv("obf", 128, NIC * 1024, BF16)
    x3 = fv("x3", 128, NDC * 1024)
    f4st = fv("f4st", 128, 2 * 1024, BF16)
    xr2 = fv("xr2", 128, 2 * 512)
    t1g = fv("t1g", 128, 512)
    sqg2 = fv("sqg2", 128, 2 * 512)
    F4F2 = fence()

    with nc.psum_tensor("psA4", [128, 512], F32) as psA, \
         nc.psum_tensor("psB4", [128, 512], F32) as psB, \
         nc.psum_tensor("psM4", [1, 512], F32) as psM, \
         nc.psum_tensor("psQ4", [1, 512], F32) as psQ, \
         nc.psum_tensor("psU4", [128, 512], F32) as psU, \
         nc.psum_tensor("psV4", [128, 512], F32) as psV:
        psAB = (psA, psB)

        def emit_ln4(xbuf, col_of, tts, ready_vv):
            mean = lnr[0:1, 0:512]
            msq = lnr[0:1, 512:1024]
            arow = lnr[0:1, 1024:1536]
            brow = lnr[0:1, 1536:2048]
            tmp = lnr[0:1, 2048:2560]
            for tt in tts:
                sqg = [C["pe"], C["pe"]]
                for oc in range(NDC):
                    par = oc % 2
                    sq = sqg2[:, par * 512:par * 512 + 512]

                    def _(t, oc=oc, tt=tt, ready_vv=ready_vv):
                        if oc == 0:
                            W(t, "vv", ready_vv)
                        I(t.matmul(psM[:, :], ones_c,
                                   xbuf[:, col_of(oc, tt):col_of(oc, tt) + 512],
                                   start=(oc == 0), stop=(oc == NDC - 1)), "pe")
                    TEN(_)

                    def _(v, sq=sq, oc=oc, tt=tt, g=sqg[par]):
                        W(v, "pe", g)
                        v.tensor_tensor(sq, xbuf[:, col_of(oc, tt):col_of(oc, tt) + 512],
                                        xbuf[:, col_of(oc, tt):col_of(oc, tt) + 512],
                                        OP.mult)
                        I(v.drain(), "vv")
                    VEC(_)
                    s1 = C["vv"]

                    def _(t, sq=sq, oc=oc, s1=s1):
                        W(t, "vv", s1)
                        I(t.matmul(psQ[:, :], ones_c, sq,
                                   start=(oc == 0), stop=(oc == NDC - 1)), "pe")
                    TEN(_)
                    sqg[par] = C["pe"]
                pes = C["pe"]

                def _(v, pes=pes):
                    W(v, "pe", pes)
                    v.tensor_scalar(mean, psM[:, :], 1.0 / 1024.0, None, OP.mult)
                    v.tensor_scalar(msq, psQ[:, :], 1.0 / 1024.0, None, OP.mult)
                    v.tensor_tensor(tmp, mean, mean, OP.mult)
                    v.tensor_tensor(msq, msq, tmp, OP.subtract)
                    PRL(v)
                    I(v.drain(), "vv")
                VEC(_)
                s1 = C["vv"]

                def _(s, s1=s1):
                    W(s, "vv", s1)
                    s.activation(tmp, msq, AF.Ln, bias=eps_t[0:1, 0:1])
                    s.activation(arow, tmp, AF.Exp, scale=-0.5)
                    I(s.drain(), "ss")
                SCA(_)
                s1 = C["ss"]

                def _(v, s1=s1):
                    W(v, "ss", s1)
                    v.scalar_tensor_tensor(brow, mean, -1.0, arow, OP.mult, OP.mult)
                    I(v.drain(), "vv")
                VEC(_)
                s1 = C["vv"]

                def _(t, s1=s1):
                    W(t, "vv", s1)
                    t.matmul(psU[:, :], ones_r, arow, start=True, stop=True)
                    I(t.matmul(psV[:, :], ones_r, brow, start=True, stop=True), "pe")
                TEN(_)
                pes = C["pe"]

                def _(v, tt=tt, pes=pes):
                    W(v, "pe", pes)
                    for oc in range(NDC):
                        csl = xbuf[:, col_of(oc, tt):col_of(oc, tt) + 512]
                        v.tensor_tensor(t1g, csl, psU[:, :], OP.mult)
                        v.tensor_tensor(csl, t1g, psV[:, :], OP.add)
                    PRL(v)
                    I(v.drain(), "vv")
                VEC(_)

        fst_g = [C["vv"], C["vv"]]
        for th in range(2):
            for ic in range(NIC):
                par = ic % 2

                def _(sync, th=th, ic=ic, par=par, g=fst_g[par]):
                    W(sync, "vv", g)
                    if th == 0 and ic < 2:
                        wfence(sync, F4F2)
                    dma(sync, f4st[:, par * 1024:(par + 1) * 1024],
                        tp_hp[ic * 128:(ic + 1) * 128, th * 1024:(th + 1) * 1024], "dl")
                SYN(_)
                ld = C["dl"]

                def _(v, ic=ic, par=par, ld=ld):
                    W(v, "dl", ld)
                    for j in range(0, 1024, 512):
                        v.tensor_scalar(qtmp[:, j:j + 512],
                                        f4st[:, par * 1024 + j:par * 1024 + j + 512],
                                        bca("r_te"), MG, OP.mult, op1=OP.add)
                        v.tensor_scalar(obf[:, ic * 1024 + j:ic * 1024 + j + 512],
                                        qtmp[:, j:j + 512], -MG, None, OP.add)
                    I(v.drain(), "vv")
                VEC(_)
                fst_g[par] = C["vv"]
            OQ_DONE = C["vv"]

            evg = [OQ_DONE, OQ_DONE]
            xrg = [OQ_DONE, OQ_DONE]
            for tt2 in range(2):
                for oc in range(NDC):
                    par = (tt2 * NDC + oc) % 2
                    ps = psAB[par]

                    def _(sync, th=th, tt2=tt2, oc=oc, par=par, g=xrg[par]):
                        W(sync, "vv", g)
                        dma(sync, xr2[:, par * 512:par * 512 + 512],
                            x2sp[oc * 128:(oc + 1) * 128,
                                 th * 1024 + tt2 * 512:th * 1024 + tt2 * 512 + 512],
                            "dl")
                    SYN(_)
                    ld = C["dl"]

                    def _(t, ps=ps, par=par, tt2=tt2, oc=oc, g=evg[par], od=OQ_DONE):
                        W(t, "vv", max(g, od))
                        for ic in range(NIC):
                            inst = t.matmul(
                                ps[:, :],
                                wdbf[:, ic * D + oc * 128:ic * D + oc * 128 + 128],
                                obf[:, ic * 1024 + tt2 * 512:ic * 1024 + tt2 * 512 + 512],
                                start=(ic == 0), stop=(ic == NIC - 1))
                        I(inst, "pe")
                    TEN(_)
                    pes = C["pe"]

                    def _(v, ps=ps, par=par, tt2=tt2, oc=oc, pes=pes, ld=ld):
                        W(v, "pe", pes)
                        W(v, "dl", ld)
                        v.tensor_scalar(t1g, ps[:, :], bca("s_o"), None, OP.mult)
                        v.tensor_tensor(
                            x3[:, oc * 1024 + tt2 * 512:oc * 1024 + tt2 * 512 + 512],
                            t1g, xr2[:, par * 512:par * 512 + 512], OP.add)
                        PRL(v)
                        I(v.drain(), "vv")
                    VEC(_)
                    evg[par] = C["vv"]
                    xrg[par] = C["vv"]
            DOWN_DONE = C["vv"]

            emit_ln4(x3, lambda oc, tt2: oc * 1024 + tt2 * 512, range(2), DOWN_DONE)
            LN2_DONE = C["vv"]

            def _(g, th=th, s1=LN2_DONE):
                W(g, "vv", s1)
                for oc in range(NDC):
                    dma(g, outT[oc * 128:(oc + 1) * 128, th * 1024:(th + 1) * 1024],
                        x3[:, oc * 1024:(oc + 1) * 1024], "dg")
            GPS(_)
            fst_g = [C["dg"], C["dg"]]
            OUT_DG = C["dg"]

            def _(v, th=th, dgv=OUT_DG):
                W(v, "dg", dgv)
                I(v.drain(), "vv")
            VEC(_)
            fst_g = [C["vv"], C["vv"]]

    def _(g):
        g.wait_ge(SEMS["dg"], C["dg"])
    GPS(_)

    return nc


_NC_CACHE = None


def kernel(**inputs):
    global _NC_CACHE
    f32 = np.float32
    bf16 = ml_dtypes.bfloat16
    x = np.asarray(inputs["L0_attn_x"], dtype=f32)
    Wq = np.asarray(inputs["L0_attn_Wq"], dtype=f32)
    Wk = np.asarray(inputs["L0_attn_Wk"], dtype=f32)
    Wv = np.asarray(inputs["L0_attn_Wv"], dtype=f32)
    Wo = np.asarray(inputs["L0_attn_Wo"], dtype=f32)
    bq = np.asarray(inputs["L0_attn_bq"], dtype=f32)
    bk = np.asarray(inputs["L0_attn_bk"], dtype=f32)
    bo = np.asarray(inputs["L0_attn_bo"], dtype=f32)
    bv = np.asarray(inputs["L0_attn_bv"], dtype=f32)
    wg = np.asarray(inputs["L0_ffn_w_gate"], dtype=f32)
    wu = np.asarray(inputs["L0_ffn_w_up"], dtype=f32)
    wd = np.asarray(inputs["L0_ffn_w_down"], dtype=f32)

    cT = np.ascontiguousarray
    common = {
        "wqT": cT(Wq.T), "wkT": cT(Wk.T), "wvT": cT(Wv.T), "woT": cT(Wo.T),
        "wuT": cT(wu.T), "wgT": cT(wg.T), "wdT": cT(wd.T),
        "wu_n": cT(wu), "wg_n": cT(wg), "wd_n": cT(wd),
        "bqpp": cT(bq.reshape(NDC, 128).T), "bkpp": cT(bk.reshape(NDC, 128).T),
        "bopp": cT(bo.reshape(NDC, 128).T), "bvrow": cT(bv.reshape(1, D)),
        "identc": np.eye(128, dtype=f32),
    }
    ind16 = np.zeros((128, NDC * 16), dtype=f32)
    for dc in range(NDC):
        ind16[0:64, dc * 16 + 2 * dc] = 1.0
        ind16[64:128, dc * 16 + 2 * dc + 1] = 1.0
    common["ind16c"] = ind16.astype(bf16)
    indk = np.zeros((96, NIC * 128), dtype=f32)
    for ic in range(NIC):
        for g in range(4):
            indk[4 * ic + g, ic * 128 + 32 * g: ic * 128 + 32 * g + 32] = 1.0
    common["indkc"] = indk

    in_maps = []
    for b in range(NCORES):
        m = dict(common)
        m["xT"] = cT(x[b].T)
        in_maps.append(m)

    if _NC_CACHE is None:
        _NC_CACHE = build_nc()
    res = run_bass_kernel_spmd(_NC_CACHE, in_maps, core_ids=list(range(NCORES)))
    out = np.stack([np.asarray(res.results[b]["outT"], dtype=f32).T
                    for b in range(NCORES)], axis=0)
    return np.ascontiguousarray(out)
